# version 01
# baseline (speedup 1.0000x reference)
import numpy as np
import ml_dtypes
from contextlib import ExitStack
import concourse.bass as bass
import concourse.mybir as mybir
from concourse.bass_utils import run_bass_kernel_spmd

F32 = mybir.dt.float32
BF16 = mybir.dt.bfloat16
AF = mybir.ActivationFunctionType
ALU = mybir.AluOpType
AX = mybir.AxisListType

B, S, D = 2, 8192, 2048
H, HD = 16, 128
NFI = D // 128
TOK = 2048
NSLOT = 16
NEXP = 20
NIC = 11
IEXP = 1408
SCALE = float(HD) ** -0.5
EPS = 1e-5
NEG_BIG = -1e30
INV_D = 1.0 / D

LAST_PHASE = "p5"
DBG = ()
P2_LIMIT = None


def _cl(v):
    return max(0, v)


def _host_prep(inp):
    f32 = np.float32
    x = np.asarray(inp["L0_attn_hidden_states"], f32)

    def wimg_sq(w):
        wt = np.ascontiguousarray(np.asarray(w, f32).T)
        return np.ascontiguousarray(
            wt.reshape(NFI, 128, NFI, 128).transpose(1, 0, 2, 3).reshape(128, NFI * D))

    wq_img = wimg_sq(inp["L0_attn_Wq"])
    wk_img = wimg_sq(inp["L0_attn_Wk"])
    wv_img = wimg_sq(inp["L0_attn_Wv"])
    wo_img = wimg_sq(inp["L0_attn_Wo"])

    gs = np.concatenate([np.asarray(inp["L0_ffn_gate_w"], f32),
                         np.asarray(inp["L0_ffn_sgate_w"], f32)], axis=0)
    gs_img = np.ascontiguousarray(
        gs.T.reshape(NFI, 128, 17).transpose(1, 0, 2).reshape(128, NFI * 17))

    w13 = np.asarray(inp["L0_ffn_w13"], f32)
    w2 = np.asarray(inp["L0_ffn_w2"], f32)
    sgu = np.asarray(inp["L0_ffn_shared_gate_up_w"], f32)
    sdw = np.asarray(inp["L0_ffn_shared_down_w"], f32)
    w13_list = [w13[e] for e in range(16)]
    w2_list = [w2[e] for e in range(16)]
    for p in range(4):
        g_p = sgu[p * IEXP:(p + 1) * IEXP]
        u_p = sgu[5632 + p * IEXP: 5632 + (p + 1) * IEXP]
        w13_list.append(np.concatenate([g_p, u_p], axis=0))
        w2_list.append(sdw[:, p * IEXP:(p + 1) * IEXP])

    w13_img = np.empty((NEXP, NIC, 128, 2 * NFI * 128), f32)
    for e in range(NEXP):
        a = w13_list[e].T.reshape(NFI, 128, 2, NIC, 128)
        w13_img[e] = a.transpose(3, 1, 2, 0, 4).reshape(NIC, 128, 2 * NFI * 128)
    w2_img = np.empty((NEXP, NFI, 128, NIC * 128), f32)
    for e in range(NEXP):
        a = w2_list[e].T.reshape(NIC, 128, NFI, 128)
        w2_img[e] = a.transpose(2, 1, 0, 3).reshape(NFI, 128, NIC * 128)

    ident = np.eye(128, dtype=f32)

    per_core = []
    for cid in range(8):
        b, j = cid // 4, cid % 4
        xT = np.ascontiguousarray(x[b].T)
        xseq = xT.reshape(NFI, 128, S)
        own_cols = ((j + 4 * (np.arange(TOK) // 128)) * 128 + np.arange(TOK) % 128)
        xown = np.ascontiguousarray(xT[:, own_cols]).reshape(NFI, 128, TOK)
        p_idx = np.arange(128)[:, None]
        cq = np.arange(4 * 128)[None, :]
        cc, qq = cq // 128, cq % 128
        maskT = np.ascontiguousarray(
            np.where(128 * cc + p_idx <= 128 * j + qq, 0.0, NEG_BIG).astype(f32))
        per_core.append({"xseq": xseq, "xown": xown, "maskT": maskT,
                         "wq": wq_img, "wk": wk_img, "wv": wv_img, "wo": wo_img,
                         "gs": gs_img, "w13": w13_img, "w2": w2_img, "ident": ident})
    return per_core


def _assemble(outs):
    y = np.empty((B, S, D), np.float32)
    for cid in range(8):
        b, j = cid // 4, cid % 4
        o = np.asarray(outs[cid])
        yt = o.reshape(NFI, 128, NSLOT, 128).transpose(2, 3, 0, 1).reshape(TOK, D)
        y[b].reshape(64, 128, D)[j::4] = yt.reshape(NSLOT, 128, D)
    return y


class _C:
    pass


class _Sem:
    __slots__ = ("h",)

    def __init__(self, h):
        self.h = h


def build_nc():
    nc = bass.Bass("TRN2", target_bir_lowering=False)
    c = _C()
    c.nc = nc
    c.t_xseq = nc.dram_tensor("xseq", [NFI, 128, S], F32, kind="ExternalInput")
    c.t_xown = nc.dram_tensor("xown", [NFI, 128, TOK], F32, kind="ExternalInput")
    c.t_mask = nc.dram_tensor("maskT", [128, 512], F32, kind="ExternalInput")
    c.t_wq = nc.dram_tensor("wq", [128, NFI * D], F32, kind="ExternalInput")
    c.t_wk = nc.dram_tensor("wk", [128, NFI * D], F32, kind="ExternalInput")
    c.t_wv = nc.dram_tensor("wv", [128, NFI * D], F32, kind="ExternalInput")
    c.t_wo = nc.dram_tensor("wo", [128, NFI * D], F32, kind="ExternalInput")
    c.t_gs = nc.dram_tensor("gs", [128, NFI * 17], F32, kind="ExternalInput")
    c.t_w13 = nc.dram_tensor("w13", [NEXP, NIC, 128, 2 * NFI * 128], F32,
                             kind="ExternalInput")
    c.t_w2 = nc.dram_tensor("w2", [NEXP, NFI, 128, NIC * 128], F32,
                            kind="ExternalInput")
    c.t_id = nc.dram_tensor("ident", [128, 128], F32, kind="ExternalInput")
    c.t_out = nc.dram_tensor("out", [NFI * 128, TOK], F32, kind="ExternalOutput")

    def scr(name, shape, dt, tag):
        kind = "ExternalOutput" if tag in DBG else "Internal"
        return nc.dram_tensor(name, shape, dt, kind=kind)

    c.t_kscr = scr("k_scr", [H, 128, S], BF16, "kv")
    c.t_vscr = scr("v_scr", [S, D], BF16, "kv")
    c.t_y1 = scr("y1_scr", [NFI, 128, TOK], F32, "y1")
    c.t_y1b = scr("y1b_scr", [NFI, 128, TOK], BF16, "y1b")
    c.t_coef = scr("coef_scr", [NEXP, TOK], BF16, "coef")
    c.t_dbg = {}
    for name, shape, dt in [("q", [128, H * TOK], BF16),
                            ("attn", [128, H * TOK], BF16),
                            ("mhat", [128, H], F32)]:
        if name in DBG:
            c.t_dbg[name] = nc.dram_tensor("dbg_" + name, shape, dt,
                                           kind="ExternalOutput")

    with ExitStack() as top:
        c.top = top
        sb = lambda n, sh, dt: top.enter_context(nc.sbuf_tensor(n, sh, dt))
        c.ones_col_bf = sb("ones_col_bf", [128, 1], BF16)
        c.ones_col_f = sb("ones_col_f", [128, 1], F32)
        c.ones_row_f = sb("ones_row_f", [1, 128], F32)
        c.ones_row_bf = sb("ones_row_bf", [1, 128], BF16)
        c.id_bf = sb("id_bf", [128, 128], BF16)
        c.mask_sb = sb("mask_sb", [128, 512], F32)
        c.mhat_neg = sb("mhat_neg", [128, H], F32)
        c.nrm_k = sb("nrm_k", [1, 256], F32)
        c.nrm_q = sb("nrm_q", [1, 64], F32)
        c.sem_i = 0

        def sem(n):
            c.sem_i += 1
            return _Sem(top.enter_context(nc.semaphore(f"{n}_{c.sem_i}")))
        c.sem = sem

        _p0_consts(c)
        if LAST_PHASE == "p0":
            return nc
        _p1_proj(c, "k")
        if LAST_PHASE == "p1k":
            return nc
        _p1_proj(c, "v")
        if LAST_PHASE == "p1v":
            return nc
        with ExitStack() as attn_scope:
            c.attn_fm = attn_scope.enter_context(
                nc.sbuf_tensor("attn_fm", [128, H * TOK], BF16))
            with ExitStack() as q_scope:
                c.q_fm = q_scope.enter_context(
                    nc.sbuf_tensor("q_fm", [128, H * TOK], BF16))
                _p1_q(c, q_scope)
                if LAST_PHASE == "p1q":
                    return nc
                _p1_mhat(c)
                _dump(c, [("q", c.q_fm), ("mhat", c.mhat_neg)])
                if LAST_PHASE == "p1":
                    return nc
                _p2_attn(c)
            _dump(c, [("attn", c.attn_fm)])
            if LAST_PHASE == "p2":
                return nc
            _p3(c)
        if LAST_PHASE == "p3":
            return nc
        _p5(c)
    return nc


def _dump(c, pairs):
    pairs = [(n, sbuf) for (n, sbuf) in pairs if n in c.t_dbg]
    if not pairs:
        return
    s = c.sem("s_dump")
    with c.nc.Block() as blk:
        @blk.sync
        def _(sy):
            n = 0
            for name, sbuf in pairs:
                sy.dma_start(c.t_dbg[name].ap(), sbuf[:, :]).then_inc(s.h, 16)
                n += 16
            sy.wait_ge(s.h, n)


def _p0_consts(c):
    nc = c.nc
    s = c.sem("s_init")
    with nc.Block() as blk:
        @blk.gpsimd
        def _(g):
            g.memset(c.ones_col_bf[:, :], 1.0)
            g.memset(c.ones_col_f[:, :], 1.0)
            g.memset(c.ones_row_f[:, :], 1.0)
            g.memset(c.ones_row_bf[:, :], 1.0)
            g.dma_start(c.id_bf[:, :], c.t_id.ap()).then_inc(s.h, 16)
            g.dma_start(c.mask_sb[:, :], c.t_mask.ap()).then_inc(s.h, 16)
            g.wait_ge(s.h, 32)


def _p1_proj(c, which):
    nc = c.nc
    NTT = S // 512
    is_k = which == "k"
    t_w = c.t_wk if is_k else c.t_wv
    t_scr = c.t_kscr if is_k else c.t_vscr
    NG = 8 if is_k else 4
    with ExitStack() as ctx:
        sbl = lambda n, sh, dt: ctx.enter_context(nc.sbuf_tensor(n, sh, dt))
        w_half = sbl(f"w_{which}", [128, 8 * NFI * 128], BF16)
        xbuf = [sbl(f"x_{which}{i}", [128, NFI * 512], BF16) for i in range(2)]
        ew = 512 if is_k else 1024
        evac = [sbl(f"ev_{which}{i}", [128, ew], BF16) for i in range(4)]
        ksq = [sbl(f"ksq{i}", [128, 512], BF16) for i in range(2)] if is_k else None
        ps = ctx.enter_context(nc.psum_tensor(f"ps_{which}", [128, 4096], F32))
        pb = lambda b: ps[:, b * 512:(b + 1) * 512]
        s_w = c.sem("sw")
        s_x = c.sem("sx")
        s_pe = c.sem("spe")
        s_ev = c.sem("sev")
        s_npe = c.sem("snpe")
        s_ndve = c.sem("sndve")
        s_st = c.sem("sst")

        def norm_mm(t, i):
            t.wait_ge(s_ev.h, i + 1)
            t.wait_ge(s_ndve.h, _cl(i - 1))
            bank = 6 + (i % 2)
            t.matmul(ps[0:1, bank * 512:(bank + 1) * 512], c.ones_col_bf[:, :],
                     ksq[i % 2][:, :], start=True, stop=True).then_inc(s_npe.h, 1)

        for hf in range(2):
            with nc.Block() as blk:
                @blk.gpsimd
                def _(g, hf=hf):
                    src = bass.AP(t_w, hf * 8 * 128,
                                  [[NFI * D, 128], [D, NFI], [1, 8 * 128]])
                    g.dma_start(w_half[:, :], src).then_inc(s_w.h, 16)
                    for tt in range(NTT):
                        gi = hf * NTT + tt
                        if gi >= 2:
                            g.wait_ge(s_pe.h, (gi - 1) * NG)
                        src = bass.AP(c.t_xseq, tt * 512,
                                      [[S, 128], [128 * S, NFI], [1, 512]])
                        g.dma_start(xbuf[gi % 2][:, :], src).then_inc(s_x.h, 16)

                @blk.tensor
                def _(t, hf=hf):
                    t.wait_ge(s_w.h, 16 * (hf + 1))
                    for tt in range(NTT):
                        gi = hf * NTT + tt
                        t.wait_ge(s_x.h, 16 * (gi + 1))
                        for cg in range(NG):
                            i = gi * NG + cg
                            t.wait_ge(s_ev.h, _cl(i - 1))
                            mm = None
                            if is_k:
                                for fi in range(NFI):
                                    mm = t.matmul(
                                        pb(i % 2),
                                        w_half[:, (fi * 8 + cg) * 128:
                                               (fi * 8 + cg) * 128 + 128],
                                        xbuf[gi % 2][:, fi * 512: fi * 512 + 512],
                                        start=(fi == 0), stop=(fi == NFI - 1))
                            else:
                                for fi in range(NFI):
                                    for b2 in range(2):
                                        mm = t.matmul(
                                            pb(2 * (i % 2) + b2),
                                            xbuf[gi % 2][:, fi * 512 + cg * 128:
                                                         fi * 512 + cg * 128 + 128],
                                            w_half[:, (fi * 8 + 4 * b2) * 128:
                                                   (fi * 8 + 4 * b2) * 128 + 512],
                                            start=(fi == 0), stop=(fi == NFI - 1))
                            mm.then_inc(s_pe.h, 1)
                            if is_k and i >= 1:
                                norm_mm(t, i - 1)
                    if is_k and hf == 1:
                        norm_mm(t, 2 * NTT * 8 - 1)

                @blk.scalar
                def _(s, hf=hf):
                    for tt in range(NTT):
                        gi = hf * NTT + tt
                        for cg in range(NG):
                            i = gi * NG + cg
                            s.wait_ge(s_pe.h, i + 1)
                            s.wait_ge(s_st.h, 16 * _cl(i - 3))
                            if is_k:
                                s.wait_ge(s_npe.h, _cl(i - 1))
                                s.copy(evac[i % 4][:, :], pb(i % 2))
                                s.activation(ksq[i % 2][:, :], pb(i % 2),
                                             AF.Square).then_inc(s_ev.h, 1)
                            else:
                                s.copy(evac[i % 4][:, 0:512], pb(2 * (i % 2)))
                                s.copy(evac[i % 4][:, 512:1024],
                                       pb(2 * (i % 2) + 1)).then_inc(s_ev.h, 1)

                if is_k:
                    @blk.vector
                    def _(v, hf=hf):
                        lo = 0 if hf == 0 else 127
                        hi = 127 if hf == 0 else 256
                        for i in range(lo, hi):
                            v.wait_ge(s_npe.h, i + 1)
                            bank = 6 + (i % 2)
                            gi2, cg2 = i // 8, i % 8
                            hf2, tt2 = gi2 // NTT, gi2 % NTT
                            slot = (hf2 * 8 + cg2) * 16 + tt2
                            v.reduce_max(c.nrm_k[:, slot:slot + 1],
                                         ps[0:1, bank * 512:(bank + 1) * 512],
                                         axis=AX.X).then_inc(s_ndve.h, 1)

                @blk.sync
                def _(sy, hf=hf):
                    for tt in range(NTT):
                        gi = hf * NTT + tt
                        for cg in range(NG):
                            i = gi * NG + cg
                            sy.wait_ge(s_ev.h, i + 1)
                            if is_k:
                                fc = hf * 8 + cg
                                dst = bass.AP(t_scr, fc * 128 * S + tt * 512,
                                              [[S, 128], [1, 512]])
                            else:
                                row0 = tt * 512 + cg * 128
                                dst = bass.AP(t_scr, row0 * D + hf * 1024,
                                              [[D, 128], [1, 1024]])
                            sy.dma_start(dst, evac[i % 4][:, :]).then_inc(s_st.h, 16)
                    if hf == 1:
                        sy.wait_ge(s_st.h, 16 * 2 * NTT * NG)


def _p1_q(c, scope):
    nc = c.nc
    NTT = 4
    with ExitStack() as ctx:
        sbl = lambda n, sh, dt: ctx.enter_context(nc.sbuf_tensor(n, sh, dt))
        w_half = sbl("w_q", [128, 8 * NFI * 128], BF16)
        xbuf = [sbl(f"x_q{i}", [128, NFI * 512], BF16) for i in range(2)]
        qsq = [sbl(f"qsq{i}", [128, 512], BF16) for i in range(2)]
        ps = ctx.enter_context(nc.psum_tensor("ps_q", [128, 4096], F32))
        pb = lambda b: ps[:, b * 512:(b + 1) * 512]
        s_w = c.sem("sqw")
        s_x = c.sem("sqx")
        s_pe = c.sem("sqpe")
        s_ev = c.sem("sqev")
        s_npe = c.sem("sqnpe")
        s_ndve = c.sem("sqndve")

        def norm_mm(t, i):
            t.wait_ge(s_ev.h, i + 1)
            t.wait_ge(s_ndve.h, _cl(i - 1))
            bank = 6 + (i % 2)
            t.matmul(ps[0:1, bank * 512:(bank + 1) * 512], c.ones_col_bf[:, :],
                     qsq[i % 2][:, :], start=True, stop=True).then_inc(s_npe.h, 1)

        for hf in range(2):
            with nc.Block() as blk:
                @blk.gpsimd
                def _(g, hf=hf):
                    src = bass.AP(c.t_wq, hf * 8 * 128,
                                  [[NFI * D, 128], [D, NFI], [1, 8 * 128]])
                    g.dma_start(w_half[:, :], src).then_inc(s_w.h, 16)
                    for tt in range(NTT):
                        gi = hf * NTT + tt
                        if gi >= 2:
                            g.wait_ge(s_pe.h, (gi - 1) * 8)
                        src = bass.AP(c.t_xown, tt * 512,
                                      [[TOK, 128], [128 * TOK, NFI], [1, 512]])
                        g.dma_start(xbuf[gi % 2][:, :], src).then_inc(s_x.h, 16)

                @blk.tensor
                def _(t, hf=hf):
                    t.wait_ge(s_w.h, 16 * (hf + 1))
                    for tt in range(NTT):
                        gi = hf * NTT + tt
                        t.wait_ge(s_x.h, 16 * (gi + 1))
                        for cg in range(8):
                            i = gi * 8 + cg
                            t.wait_ge(s_ev.h, _cl(i - 1))
                            mm = None
                            for fi in range(NFI):
                                mm = t.matmul(
                                    pb(i % 2),
                                    w_half[:, (fi * 8 + cg) * 128:
                                           (fi * 8 + cg) * 128 + 128],
                                    xbuf[gi % 2][:, fi * 512: fi * 512 + 512],
                                    start=(fi == 0), stop=(fi == NFI - 1))
                            mm.then_inc(s_pe.h, 1)
                            if i >= 1:
                                norm_mm(t, i - 1)
                    if hf == 1:
                        norm_mm(t, 2 * NTT * 8 - 1)

                @blk.scalar
                def _(s, hf=hf):
                    for tt in range(NTT):
                        gi = hf * NTT + tt
                        for cg in range(8):
                            i = gi * 8 + cg
                            fc = hf * 8 + cg
                            s.wait_ge(s_pe.h, i + 1)
                            s.wait_ge(s_npe.h, _cl(i - 1))
                            s.activation(
                                c.q_fm[:, fc * TOK + tt * 512:
                                       fc * TOK + tt * 512 + 512],
                                pb(i % 2), AF.Copy, scale=SCALE)
                            s.activation(qsq[i % 2][:, :], pb(i % 2),
                                         AF.Square, scale=SCALE).then_inc(s_ev.h, 1)

                @blk.vector
                def _(v, hf=hf):
                    lo = 0 if hf == 0 else 31
                    hi = 31 if hf == 0 else 64
                    for i in range(lo, hi):
                        v.wait_ge(s_npe.h, i + 1)
                        bank = 6 + (i % 2)
                        gi2, cg2 = i // 8, i % 8
                        hf2, tt2 = gi2 // NTT, gi2 % NTT
                        slot = (hf2 * 8 + cg2) * 4 + tt2
                        v.reduce_max(c.nrm_q[:, slot:slot + 1],
                                     ps[0:1, bank * 512:(bank + 1) * 512],
                                     axis=AX.X).then_inc(s_ndve.h, 1)


def _p1_mhat(c):
    nc = c.nc
    with ExitStack() as ctx:
        sbl = lambda n, sh, dt: ctx.enter_context(nc.sbuf_tensor(n, sh, dt))
        km = sbl("km2", [1, H], F32)
        qm = sbl("qm2", [1, H], F32)
        m2 = sbl("m2", [1, H], F32)
        mh = sbl("mh", [1, H], F32)
        mhn = sbl("mhn", [1, H], F32)
        ps = ctx.enter_context(nc.psum_tensor("ps_mh", [128, 512], F32))
        s_a = c.sem("sma")
        s_b = c.sem("smb")
        s_d = c.sem("smd")
        with nc.Block() as blk:
            @blk.vector
            def _(v):
                mm = None
                for fc in range(H):
                    v.reduce_max(km[:, fc:fc + 1],
                                 c.nrm_k[0:1, fc * 16: fc * 16 + 16], axis=AX.X)
                    mm = v.reduce_max(qm[:, fc:fc + 1],
                                      c.nrm_q[0:1, fc * 4: fc * 4 + 4],
                                      axis=AX.X)
                mm.then_inc(s_a.h, 1)
                v.wait_ge(s_a.h, 1)
                v.tensor_mul(m2[:, :], km[:, :], qm[:, :]).then_inc(s_a.h, 1)

            @blk.scalar
            def _(s):
                s.wait_ge(s_a.h, 2)
                s.activation(mh[:, :], m2[:, :], AF.Sqrt).then_inc(s_b.h, 1)
                s.wait_ge(s_b.h, 1)
                s.activation(mhn[:, :], mh[:, :], AF.Copy,
                             scale=-1.0).then_inc(s_b.h, 2)

            @blk.tensor
            def _(t):
                t.wait_ge(s_b.h, 3)
                t.matmul(ps[:, 0:H], c.ones_row_f[:, :], mhn[:, :],
                         start=True, stop=True).then_inc(s_d.h, 1)
        with nc.Block() as blk2:
            @blk2.scalar
            def _(s):
                s.copy(c.mhat_neg[:, :], ps[:, 0:H])


def _p2_schedule():
    units, ktloads, ghps = [], [], []
    u = kl = 0
    for g in range(4):
        for hp in range(8):
            ghp = {"g": g, "hp": hp, "idx": len(ghps), "u0": u, "units": [],
                   "kls": []}
            for kt in range(4 * g + 4):
                rec = {"g": g, "hp": hp, "kt": kt, "idx": kl, "u0": u}
                for qb in range(4 * g, 4 * g + 4):
                    if qb < kt:
                        continue
                    for h2 in range(2):
                        un = {"idx": u, "g": g, "hp": hp, "kt": kt, "qb": qb,
                              "h2": h2, "kl": kl, "diag": kt == qb,
                              "first": kt == 0, "last": kt == qb,
                              "r": (qb - 4 * g) * 2 + h2}
                        units.append(un)
                        ghp["units"].append(un)
                        u += 1
                rec["u1"] = u
                ktloads.append(rec)
                ghp["kls"].append(rec)
                kl += 1
            ghp["u1"] = u
            ghps.append(ghp)
    nd = 0
    for un in units:
        if un["diag"]:
            nd += 1
        un["ndiag"] = nd
    return units, ktloads, ghps


def _p2_attn(c):
    nc = c.nc
    units, ktloads, ghps = _p2_schedule()
    if P2_LIMIT is not None:
        ghps = ghps[:P2_LIMIT]
        units = [u for g in ghps for u in g["units"]]
        kls = set()
        for g in ghps:
            for r in g["kls"]:
                kls.add(r["idx"])
        ktloads = [r for r in ktloads if r["idx"] in kls]
    with ExitStack() as ctx:
        sbl = lambda n, sh, dt: ctx.enter_context(nc.sbuf_tensor(n, sh, dt))
        kbuf = [sbl(f"kbuf{i}", [128, 1024], BF16) for i in range(2)]
        vbuf = [sbl(f"vbuf{i}", [128, 1024], BF16) for i in range(2)]
        ptb = [sbl(f"ptb{i}", [128, 512], BF16) for i in range(2)]
        linv = sbl("linv", [1, 1024], F32)
        linv_rep = sbl("linv_rep", [128, 1024], F32)
        ps = ctx.enter_context(nc.psum_tensor("ps_at", [128, 4096], F32))
        s_kv = c.sem("skv")
        s_spe = c.sem("sspe")
        s_mask = c.sem("smask")
        s_exp = c.sem("sexp")
        s_av = c.sem("sav")
        s_rec = c.sem("srec")
        s_bc = c.sem("sbc")
        s_lr = c.sem("slr")
        s_fin = c.sem("sfin")

        st_ps = lambda u: ps[:, (u % 2) * 512:(u % 2) * 512 + 512]

        def av_ps(r):
            off = (4 + r // 4) * 512 + (r % 4) * 128
            return ps[:, off: off + 128]

        def l_ps(r):
            off = (6 + r // 4) * 512 + (r % 4) * 128
            return ps[0:1, off: off + 128]

        def bc_ps(r):
            off = (2 + r % 2) * 512 + (r // 2) * 128
            return ps[:, off: off + 128]

        def lav(t, un):
            u, r, h2 = un["idx"], un["r"], un["h2"]
            t.wait_ge(s_exp.h, u + 1)
            first, last = un["first"], un["last"]
            st = first and (r % 4 == 0)
            for cc in range(4):
                t.matmul(l_ps(r), c.ones_col_bf[:, :],
                         ptb[u % 2][:, cc * 128: cc * 128 + 128],
                         start=(st and cc == 0), stop=(last and cc == 3))
            mm = None
            for cc in range(4):
                mm = t.matmul(av_ps(r),
                              vbuf[un["kl"] % 2][:, cc * 256 + h2 * 128:
                                                 cc * 256 + h2 * 128 + 128],
                              ptb[u % 2][:, cc * 128: cc * 128 + 128],
                              start=(st and cc == 0), stop=(last and cc == 3))
            mm.then_inc(s_av.h, 1)

        with nc.Block() as blk:
            @blk.sync
            def _(sy):
                for rec in ktloads:
                    i = rec["idx"]
                    if i >= 2:
                        sy.wait_ge(s_av.h, ktloads[i - 2]["u1"])
                    hp, kt = rec["hp"], rec["kt"]
                    srck = bass.AP(c.t_kscr, (2 * hp) * 128 * S + kt * 512,
                                   [[S, 128], [128 * S, 2], [1, 512]])
                    sy.dma_start(kbuf[i % 2][:, :], srck).then_inc(s_kv.h, 16)
                    srcv = bass.AP(c.t_vscr, (kt * 512) * D + hp * 256,
                                   [[D, 128], [128 * D, 4], [1, 256]])
                    sy.dma_start(vbuf[i % 2][:, :], srcv).then_inc(s_kv.h, 16)

            @blk.tensor
            def _(t):
                for ghp in ghps:
                    for un in ghp["units"]:
                        u = un["idx"]
                        if u == ghp["u0"]:
                            t.wait_ge(s_rec.h, 8 * ghp["idx"])
                            t.wait_ge(s_fin.h, 8 * ghp["idx"])
                        t.wait_ge(s_kv.h, 32 * (un["kl"] + 1))
                        t.wait_ge(s_exp.h, _cl(u - 1))
                        hg = 2 * un["hp"] + un["h2"]
                        qb = un["qb"]
                        mm = None
                        for cc in range(4):
                            mm = t.matmul(
                                st_ps(u)[:, cc * 128:cc * 128 + 128],
                                kbuf[un["kl"] % 2][:, un["h2"] * 512 + cc * 128:
                                                   un["h2"] * 512 + cc * 128 + 128],
                                c.q_fm[:, hg * TOK + qb * 128:
                                       hg * TOK + qb * 128 + 128],
                                start=True, stop=True)
                        mm.then_inc(s_spe.h, 1)
                        if u > ghp["u0"]:
                            lav(t, units[u - 1])
                    lav(t, units[ghp["u1"] - 1])
                    for r2 in range(8):
                        t.wait_ge(s_rec.h, 8 * ghp["idx"] + r2 + 1)
                        t.wait_ge(s_lr.h, 8 * ghp["idx"] + _cl(r2 - 1))
                        t.matmul(bc_ps(r2), c.ones_row_f[:, :],
                                 linv[0:1, r2 * 128: r2 * 128 + 128],
                                 start=True, stop=True).then_inc(s_bc.h, 1)

            @blk.vector
            def _(v):
                for ghp in ghps:
                    for un in ghp["units"]:
                        if not un["diag"]:
                            continue
                        u = un["idx"]
                        v.wait_ge(s_spe.h, u + 1)
                        v.tensor_add(st_ps(u), st_ps(u),
                                     c.mask_sb[:, :]).then_inc(s_mask.h, 1)
                    v.wait_ge(s_av.h, ghp["u1"])
                    for r in range(8):
                        v.reciprocal(linv[0:1, r * 128: r * 128 + 128],
                                     l_ps(r)).then_inc(s_rec.h, 1)
                    for r in range(8):
                        v.wait_ge(s_lr.h, 8 * ghp["idx"] + r + 1)
                        qb = 4 * ghp["g"] + r // 2
                        hg = 2 * ghp["hp"] + r % 2
                        v.tensor_mul(
                            c.attn_fm[:, hg * TOK + qb * 128:
                                      hg * TOK + qb * 128 + 128],
                            av_ps(r),
                            linv_rep[:, r * 128: r * 128 + 128]).then_inc(s_fin.h, 1)

            @blk.scalar
            def _(s):
                for ghp in ghps:
                    for un in ghp["units"]:
                        u = un["idx"]
                        hg = 2 * un["hp"] + un["h2"]
                        if un["diag"]:
                            s.wait_ge(s_mask.h, un["ndiag"])
                        else:
                            s.wait_ge(s_spe.h, u + 1)
                        s.wait_ge(s_av.h, _cl(u - 1))
                        s.activation(ptb[u % 2][:, :], st_ps(u), AF.Exp,
                                     bias=c.mhat_neg[:, hg:hg + 1]
                                     ).then_inc(s_exp.h, 1)
                    for r in range(8):
                        s.wait_ge(s_bc.h, 8 * ghp["idx"] + r + 1)
                        s.copy(linv_rep[:, r * 128: r * 128 + 128],
                               bc_ps(r)).then_inc(s_lr.h, 1)


def _p3(c):
    nc = c.nc
    with ExitStack() as ctx:
        sbl = lambda n, sh, dt: ctx.enter_context(nc.sbuf_tensor(n, sh, dt))
        wobuf = [sbl(f"wob{i}", [128, NFI * 128], BF16) for i in range(2)]
        gs_sb = sbl("gs_sb", [128, NFI * 17], F32)
        ypre = sbl("ypre3", [128, NFI * 512], F32)
        y1f = sbl("y1f3", [128, NFI * 512], F32)
        y1b = sbl("y1b3", [128, NFI * 512], BF16)
        stat = sbl("stat3", [1, 512 * 6], F32)
        probs = sbl("probs3", [128, 16], F32)
        prn = sbl("prn3", [128, 16], F32)
        wk4 = [sbl(f"wk43{i}", [128, 16], F32) for i in range(2)]
        mx = sbl("mx3", [128, 1], F32)
        nmx = sbl("nmx3", [128, 1], F32)
        eq = sbl("eq3", [128, 16], F32)
        coef_f = sbl("coef_f3", [128, 16], F32)
        coef_bf = sbl("coef_bf3", [128, 17], BF16)
        crow_st = [sbl(f"crow3{i}", [1, NEXP * 128], BF16) for i in range(2)]
        sgc = sbl("sgc3", [128, 1], F32)
        lsum = sbl("lsum3", [128, 1], F32)
        lin = sbl("lin3", [128, 1], F32)
        ps = ctx.enter_context(nc.psum_tensor("ps_p3", [128, 2048], F32))
        ps_sum = ctx.enter_context(nc.psum_tensor("ps_sum3", [1, 512], F32))
        ps_sq = ctx.enter_context(nc.psum_tensor("ps_sq3", [1, 512], F32))
        ps_rt = ctx.enter_context(nc.psum_tensor("ps_rt3", [128, 68], F32))
        pb = lambda b: ps[:, b * 512:(b + 1) * 512]
        ps_tr = ps[0:1, 0:128]

        s_w = c.sem("s3w")
        s_gs = c.sem("s3gs")
        s_x = c.sem("s3x")
        s_ope = c.sem("s3ope")
        s_yp = c.sem("s3yp")
        s_sq = c.sem("s3sq")
        s_stpe = c.sem("s3stpe")
        s_nrm = c.sem("s3nrm")
        s_rstd = c.sem("s3rstd")
        s_bc = c.sem("s3bc")
        s_sub = c.sem("s3sub")
        s_y1 = c.sem("s3y1")
        s_y1b = c.sem("s3y1b")
        s_st = c.sem("s3st")
        s_rpe = c.sem("s3rpe")
        s_sm = c.sem("s3sm")
        s_cst = c.sem("s3cst")
        c.sm_cnt = 0

        with nc.Block() as blk:
            @blk.gpsimd
            def _(g):
                g.dma_start(gs_sb[:, :], c.t_gs.ap()).then_inc(s_gs.h, 16)
                g.wait_ge(s_gs.h, 16)

        for qt in range(4):
            with nc.Block() as blk:
                @blk.gpsimd
                def _(g, qt=qt):
                    for fc in range(NFI):
                        i = qt * NFI + fc
                        if i >= 2:
                            g.wait_ge(s_ope.h, i - 1)
                        src = bass.AP(c.t_wo, fc * 128,
                                      [[NFI * D, 128], [D, NFI], [1, 128]])
                        g.dma_start(wobuf[fc % 2][:, :], src).then_inc(s_w.h, 16)

                @blk.sync
                def _(sy, qt=qt):
                    src = bass.AP(c.t_xown, qt * 512,
                                  [[TOK, 128], [128 * TOK, NFI], [1, 512]])
                    sy.dma_start(ypre[:, :], src).then_inc(s_x.h, 16)

                @blk.tensor
                def _(t, qt=qt):
                    t.wait_ge(s_x.h, 16 * (qt + 1))
                    for fc in range(NFI):
                        i = qt * NFI + fc
                        t.wait_ge(s_w.h, 16 * (i + 1))
                        t.wait_ge(s_yp.h, _cl(i - 1))
                        mm = None
                        for fi in range(NFI):
                            mm = t.matmul(
                                pb(i % 2),
                                wobuf[fc % 2][:, fi * 128: fi * 128 + 128],
                                c.attn_fm[:, fi * TOK + qt * 512:
                                          fi * TOK + qt * 512 + 512],
                                start=(fi == 0), stop=(fi == NFI - 1))
                        mm.then_inc(s_ope.h, 1)

                @blk.vector
                def _(v, qt=qt):
                    for fc in range(NFI):
                        i = qt * NFI + fc
                        v.wait_ge(s_ope.h, i + 1)
                        v.tensor_add(ypre[:, fc * 512: fc * 512 + 512],
                                     pb(i % 2),
                                     ypre[:, fc * 512: fc * 512 + 512]
                                     ).then_inc(s_yp.h, 1)

            nrm0 = qt * 5
            with nc.Block() as blk:
                @blk.scalar
                def _(s, qt=qt):
                    mm = None
                    for fc in range(NFI):
                        mm = s.activation(y1f[:, fc * 512: fc * 512 + 512],
                                          ypre[:, fc * 512: fc * 512 + 512],
                                          AF.Square)
                    mm.then_inc(s_sq.h, 1)
                    s.wait_ge(s_nrm.h, nrm0 + 5)
                    s.activation(stat[0:1, 2560:3072], stat[0:1, 2048:2560],
                                 AF.Sqrt).then_inc(s_rstd.h, 1)
                    s.wait_ge(s_y1.h, qt + 1)
                    mm = None
                    for fc in range(NFI):
                        mm = s.copy(y1b[:, fc * 512: fc * 512 + 512],
                                    y1f[:, fc * 512: fc * 512 + 512])
                    mm.then_inc(s_y1b.h, 1)

                @blk.tensor
                def _(t, qt=qt):
                    for fc in range(NFI):
                        t.matmul(ps_sum[0:1, :], c.ones_col_f[:, :],
                                 ypre[:, fc * 512: fc * 512 + 512],
                                 start=(fc == 0), stop=(fc == NFI - 1))
                    t.wait_ge(s_sq.h, qt + 1)
                    mm = None
                    for fc in range(NFI):
                        mm = t.matmul(ps_sq[0:1, :], c.ones_col_f[:, :],
                                      y1f[:, fc * 512: fc * 512 + 512],
                                      start=(fc == 0), stop=(fc == NFI - 1))
                    mm.then_inc(s_stpe.h, 1)
                    t.wait_ge(s_nrm.h, nrm0 + 1)
                    t.matmul(pb(2), c.ones_row_f[:, :], stat[0:1, 0:512],
                             start=True, stop=True)
                    t.wait_ge(s_rstd.h, qt + 1)
                    t.matmul(pb(3), c.ones_row_f[:, :], stat[0:1, 2560:3072],
                             start=True, stop=True).then_inc(s_bc.h, 1)
                    t.wait_ge(s_y1.h, qt + 1)
                    mm = None
                    for fc in range(NFI):
                        for tc in range(4):
                            mm = t.matmul(ps_rt[:, tc * 17: tc * 17 + 17],
                                          y1f[:, fc * 512 + tc * 128:
                                              fc * 512 + tc * 128 + 128],
                                          gs_sb[:, fc * 17: fc * 17 + 17],
                                          start=(fc == 0 and tc == 0),
                                          stop=(fc == NFI - 1))
                    mm.then_inc(s_rpe.h, 1)

                @blk.vector
                def _(v, qt=qt):
                    v.wait_ge(s_stpe.h, qt + 1)
                    v.tensor_scalar_mul(stat[0:1, 0:512], ps_sum[0:1, :], INV_D)
                    v.tensor_scalar_mul(stat[0:1, 512:1024], ps_sq[0:1, :],
                                        INV_D).then_inc(s_nrm.h, 1)
                    v.wait_ge(s_nrm.h, nrm0 + 1)
                    v.tensor_mul(stat[0:1, 1024:1536], stat[0:1, 0:512],
                                 stat[0:1, 0:512]).then_inc(s_nrm.h, 1)
                    v.wait_ge(s_nrm.h, nrm0 + 2)
                    v.tensor_sub(stat[0:1, 1536:2048], stat[0:1, 512:1024],
                                 stat[0:1, 1024:1536]).then_inc(s_nrm.h, 1)
                    v.wait_ge(s_nrm.h, nrm0 + 3)
                    v.tensor_scalar_add(stat[0:1, 2048:2560],
                                        stat[0:1, 1536:2048],
                                        EPS).then_inc(s_nrm.h, 1)
                    v.wait_ge(s_nrm.h, nrm0 + 4)
                    v.reciprocal(stat[0:1, 2048:2560],
                                 stat[0:1, 2048:2560]).then_inc(s_nrm.h, 1)
                    v.wait_ge(s_bc.h, qt + 1)
                    mm = None
                    for fc in range(NFI):
                        mm = v.tensor_sub(y1f[:, fc * 512: fc * 512 + 512],
                                          ypre[:, fc * 512: fc * 512 + 512],
                                          pb(2))
                    mm.then_inc(s_sub.h, 1)
                    v.wait_ge(s_sub.h, qt + 1)
                    mm = None
                    for fc in range(NFI):
                        mm = v.tensor_mul(y1f[:, fc * 512: fc * 512 + 512],
                                          y1f[:, fc * 512: fc * 512 + 512],
                                          pb(3))
                    mm.then_inc(s_y1.h, 1)

                @blk.sync
                def _(sy, qt=qt):
                    sy.wait_ge(s_y1.h, qt + 1)
                    dst = bass.AP(c.t_y1, qt * 512,
                                  [[TOK, 128], [128 * TOK, NFI], [1, 512]])
                    sy.dma_start(dst, y1f[:, :]).then_inc(s_st.h, 16)
                    sy.wait_ge(s_y1b.h, qt + 1)
                    dstb = bass.AP(c.t_y1b, qt * 512,
                                   [[TOK, 128], [128 * TOK, NFI], [1, 512]])
                    sy.dma_start(dstb, y1b[:, :]).then_inc(s_st.h, 16)
                    sy.wait_ge(s_st.h, 32 * (qt + 1))

            steps = []
            tc_dma = []

            def V(fn):
                steps.append(("v", fn))

            def A(fn):
                steps.append(("s", fn))

            def T(fn):
                steps.append(("t", fn))

            for tc in range(4):
                r16 = ps_rt[:, tc * 17: tc * 17 + 16]
                rsg = ps_rt[:, tc * 17 + 16: tc * 17 + 17]
                V(lambda v, r16=r16: v.reduce_max(nmx[:, :], r16, axis=AX.X,
                                                  negate=True))
                A(lambda s, r16=r16: s.activation(probs[:, :], r16, AF.Exp,
                                                  bias=nmx[:, :],
                                                  accum_out=lsum[:, :]))
                A(lambda s, rsg=rsg: s.activation(sgc[:, :], rsg, AF.Sigmoid))
                V(lambda v: v.reciprocal(lin[:, :], lsum[:, :]))
                V(lambda v: v.tensor_scalar_mul(prn[:, :], probs[:, :],
                                                lin[:, :]))
                for it in range(4):
                    src = prn if it == 0 else wk4[it % 2]
                    V(lambda v, src=src: v.reduce_max(mx[:, :], src[:, :],
                                                      axis=AX.X))
                    if it < 3:
                        V(lambda v, src=src: v.tensor_scalar(
                            eq[:, :], src[:, :], mx[:, :], None, ALU.is_ge))
                        V(lambda v, src=src, it=it: v.scalar_tensor_tensor(
                            wk4[(it + 1) % 2][:, :], eq[:, :], -1e9, src[:, :],
                            ALU.mult, ALU.add))
                V(lambda v: v.tensor_scalar(eq[:, :], prn[:, :], mx[:, :],
                                            None, ALU.is_ge))
                V(lambda v: v.tensor_mul(coef_f[:, :], prn[:, :], eq[:, :]))
                V(lambda v: v.tensor_copy(coef_bf[:, 0:16], coef_f[:, :]))
                V(lambda v: v.tensor_copy(coef_bf[:, 16:17], sgc[:, :]))
                tcg = qt * 4 + tc
                firstcopy = [True]
                for e in range(17):
                    col = e if e < 16 else 16
                    T(lambda t, col=col: t.matmul(
                        ps_tr, coef_bf[:, col:col + 1], c.id_bf[:, :],
                        start=True, stop=True))
                    dsts = [e] if e < 16 else [16, 17, 18, 19]
                    for de in dsts:
                        fc_ = firstcopy[0]
                        firstcopy[0] = False

                        def cp(s, de=de, tcg=tcg, fc_=fc_):
                            if fc_:
                                s.wait_ge(s_cst.h, 16 * _cl(tcg - 1))
                            return s.copy(
                                crow_st[tcg % 2][0:1, de * 128: de * 128 + 128],
                                ps_tr)
                        A(cp)
                copy_end = len(steps)
                tc_dma.append((tcg, qt, tc, copy_end))

            base = c.sm_cnt
            with nc.Block() as blk:
                @blk.vector
                def _(v, steps=steps, base=base):
                    for idx, (eng, fn) in enumerate(steps):
                        if eng == "v":
                            v.wait_ge(s_sm.h, base + idx)
                            fn(v).then_inc(s_sm.h, 1)

                @blk.scalar
                def _(s, steps=steps, base=base):
                    for idx, (eng, fn) in enumerate(steps):
                        if eng == "s":
                            s.wait_ge(s_sm.h, base + idx)
                            fn(s).then_inc(s_sm.h, 1)

                @blk.tensor
                def _(t, steps=steps, base=base):
                    for idx, (eng, fn) in enumerate(steps):
                        if eng == "t":
                            t.wait_ge(s_sm.h, base + idx)
                            fn(t).then_inc(s_sm.h, 1)

                @blk.sync
                def _(sy, tc_dma=tc_dma, base=base):
                    for tcg, qt2, tc2, copy_end in tc_dma:
                        sy.wait_ge(s_sm.h, base + copy_end)
                        dst = bass.AP(c.t_coef, qt2 * 512 + tc2 * 128,
                                      [[TOK, NEXP], [1, 128]])
                        sy.dma_start(dst, crow_st[tcg % 2][0:1, :]
                                     ).then_inc(s_cst.h, 16)
                    sy.wait_ge(s_cst.h, 16 * (tc_dma[-1][0] + 1))
            c.sm_cnt += len(steps)


def _p5(c):
    nc = c.nc
    s_y1bh = c.sem("s5y")
    s_w13 = c.sem("s5w13")
    s_gupe = c.sem("s5gupe")
    s_silu = c.sem("s5silu")
    s_uc = c.sem("s5uc")
    s_h = c.sem("s5h")
    s_cpe = c.sem("s5cpe")
    s_crep = c.sem("s5crep")
    s_w2 = c.sem("s5w2")
    s_w2pe = c.sem("s5w2pe")
    s_acc = c.sem("s5acc")
    s_crow = c.sem("s5crow")
    s_res = c.sem("s5res")
    s_add = c.sem("s5add")
    s_sq = c.sem("s5sq")
    s_stpe = c.sem("s5stpe")
    s_nrm = c.sem("s5nrm")
    s_rstd = c.sem("s5rstd")
    s_bc = c.sem("s5bc")
    s_sub = c.sem("s5sub")
    s_y2 = c.sem("s5y2")
    s_out = c.sem("s5out")

    for th in range(2):
        with ExitStack() as thc:
            sbl = lambda n, sh, dt: thc.enter_context(nc.sbuf_tensor(n, sh, dt))
            y1bh = sbl(f"y1bh{th}", [128, NFI * 1024], BF16)
            acc = sbl(f"acc5{th}", [128, NFI * 1024], F32)
            with ExitStack() as ectx:
                sbe = lambda n, sh, dt: ectx.enter_context(
                    nc.sbuf_tensor(n, sh, dt))
                w13buf = [sbe(f"w13b{th}_{i}", [128, 2 * NFI * 128], BF16)
                          for i in range(2)]
                w2buf = [sbe(f"w2b{th}_{i}", [128, NIC * 128], BF16)
                         for i in range(2)]
                h_sb = sbe(f"h5_{th}", [128, NIC * 1024], BF16)
                sgs = [sbe(f"sg5{th}_{i}", [128, 512], BF16) for i in range(4)]
                ucs = [sbe(f"uc5{th}_{i}", [128, 512], BF16) for i in range(4)]
                crep = sbe(f"crep5_{th}", [128, 1024], BF16)
                crow = [sbe(f"crow5_{th}_{i}", [1, 1024], BF16)
                        for i in range(2)]
                ps = ectx.enter_context(nc.psum_tensor(f"ps5_{th}",
                                                       [128, 4096], F32))
                pb = lambda b: ps[:, b * 512:(b + 1) * 512]

                with nc.Block() as blk:
                    @blk.sync
                    def _(sy, th=th):
                        src = bass.AP(c.t_y1b, th * 1024,
                                      [[TOK, 128], [128 * TOK, NFI], [1, 1024]])
                        sy.dma_start(y1bh[:, :], src).then_inc(s_y1bh.h, 16)
                        for e in range(NEXP):
                            eg = th * NEXP + e
                            if eg >= 2:
                                sy.wait_ge(s_cpe.h, 2 * (eg - 1))
                            src = bass.AP(c.t_coef, e * TOK + th * 1024,
                                          [[1, 1], [1, 1024]])
                            sy.dma_start(crow[e % 2][0:1, :],
                                         src).then_inc(s_crow.h, 16)

                    @blk.gpsimd
                    def _(g, th=th):
                        for e in range(NEXP):
                            eg = th * NEXP + e
                            for ic in range(NIC):
                                gic = eg * NIC + ic
                                if gic >= 2:
                                    g.wait_ge(s_gupe.h, gic - 1)
                                src = bass.AP(
                                    c.t_w13,
                                    (e * NIC + ic) * 128 * 2 * NFI * 128,
                                    [[2 * NFI * 128, 128], [1, 2 * NFI * 128]])
                                g.dma_start(w13buf[ic % 2][:, :],
                                            src).then_inc(s_w13.h, 16)
                            for fc in range(NFI):
                                gfc = eg * NFI + fc
                                if gfc >= 2:
                                    g.wait_ge(s_w2pe.h, 2 * (gfc - 1))
                                src = bass.AP(
                                    c.t_w2,
                                    (e * NFI + fc) * 128 * NIC * 128,
                                    [[NIC * 128, 128], [1, NIC * 128]])
                                g.dma_start(w2buf[fc % 2][:, :],
                                            src).then_inc(s_w2.h, 16)

                    @blk.tensor
                    def _(t, th=th):
                        t.wait_ge(s_y1bh.h, 16 * (th + 1))
                        for e in range(NEXP):
                            eg = th * NEXP + e
                            t.wait_ge(s_silu.h, 22 * eg)
                            t.wait_ge(s_uc.h, 22 * eg)
                            t.wait_ge(s_acc.h, _cl(32 * eg - 6))
                            t.wait_ge(s_crow.h, 16 * (eg + 1))
                            for tt2 in range(2):
                                t.matmul(pb(tt2), c.ones_row_bf[:, :],
                                         crow[e % 2][0:1, tt2 * 512:
                                                     tt2 * 512 + 512],
                                         start=True,
                                         stop=True).then_inc(s_cpe.h, 1)
                            for ic in range(NIC):
                                gic = eg * NIC + ic
                                p4 = 4 * (ic % 2)
                                t.wait_ge(s_w13.h, 16 * (gic + 1))
                                if ic == 0:
                                    t.wait_ge(s_crep.h, 2 * (eg + 1))
                                    t.wait_ge(s_acc.h, _cl(32 * eg - 4))
                                elif ic == 1:
                                    t.wait_ge(s_acc.h, _cl(32 * eg))
                                else:
                                    t.wait_ge(s_silu.h, 2 * (gic - 1))
                                    t.wait_ge(s_uc.h, 2 * (gic - 1))
                                mm = None
                                for gu in range(2):
                                    for fi in range(NFI):
                                        for tt2 in range(2):
                                            mm = t.matmul(
                                                pb(p4 + 2 * gu + tt2),
                                                w13buf[ic % 2][
                                                    :, (gu * NFI + fi) * 128:
                                                    (gu * NFI + fi) * 128 + 128],
                                                y1bh[:, fi * 1024 + tt2 * 512:
                                                     fi * 1024 + tt2 * 512 + 512],
                                                start=(fi == 0),
                                                stop=(fi == NFI - 1))
                                mm.then_inc(s_gupe.h, 1)
                            t.wait_ge(s_h.h, 22 * (eg + 1))
                            for fc in range(NFI):
                                gfc = eg * NFI + fc
                                t.wait_ge(s_w2.h, 16 * (gfc + 1))
                                for tt2 in range(2):
                                    gw = 2 * gfc + tt2
                                    bk = gw % 8
                                    t.wait_ge(s_acc.h, _cl(gw - 7))
                                    mm = None
                                    for ic in range(NIC):
                                        mm = t.matmul(
                                            pb(bk),
                                            w2buf[fc % 2][:, ic * 128:
                                                          ic * 128 + 128],
                                            h_sb[:, ic * 1024 + tt2 * 512:
                                                 ic * 1024 + tt2 * 512 + 512],
                                            start=(ic == 0), stop=(ic == NIC - 1))
                                    mm.then_inc(s_w2pe.h, 1)

                    @blk.scalar
                    def _(s, th=th):
                        for e in range(NEXP):
                            eg = th * NEXP + e
                            for tt2 in range(2):
                                s.wait_ge(s_cpe.h, 2 * eg + tt2 + 1)
                                if tt2 == 0:
                                    s.wait_ge(s_uc.h, 22 * eg)
                                s.copy(crep[:, tt2 * 512: tt2 * 512 + 512],
                                       pb(tt2)).then_inc(s_crep.h, 1)
                            for ic in range(NIC):
                                gic = eg * NIC + ic
                                p4 = 4 * (ic % 2)
                                s.wait_ge(s_gupe.h, gic + 1)
                                s.wait_ge(s_h.h, _cl(2 * (gic - 1)))
                                for tt2 in range(2):
                                    s.activation(sgs[(2 * gic + tt2) % 4][:, :],
                                                 pb(p4 + tt2), AF.Silu
                                                 ).then_inc(s_silu.h, 1)
                            if e == 0:
                                for fc in range(NFI):
                                    gfc = eg * NFI + fc
                                    for tt2 in range(2):
                                        gw = 2 * gfc + tt2
                                        s.wait_ge(s_w2pe.h, gw + 1)
                                        s.copy(acc[:, fc * 1024 + tt2 * 512:
                                                   fc * 1024 + tt2 * 512 + 512],
                                               pb(gw % 8)).then_inc(s_acc.h, 1)

                    @blk.vector
                    def _(v, th=th):
                        for e in range(NEXP):
                            eg = th * NEXP + e
                            for ic in range(NIC):
                                gic = eg * NIC + ic
                                p4 = 4 * (ic % 2)
                                v.wait_ge(s_gupe.h, gic + 1)
                                if ic < 2:
                                    v.wait_ge(s_w2pe.h, 32 * eg)
                                v.wait_ge(s_h.h, _cl(2 * (gic - 1)))
                                for tt2 in range(2):
                                    v.tensor_mul(ucs[(2 * gic + tt2) % 4][:, :],
                                                 crep[:, tt2 * 512:
                                                      tt2 * 512 + 512],
                                                 pb(p4 + 2 + tt2)
                                                 ).then_inc(s_uc.h, 1)
                                v.wait_ge(s_uc.h, 2 * (gic + 1))
                                v.wait_ge(s_silu.h, 2 * (gic + 1))
                                for tt2 in range(2):
                                    v.tensor_mul(h_sb[:, ic * 1024 + tt2 * 512:
                                                      ic * 1024 + tt2 * 512 + 512],
                                                 sgs[(2 * gic + tt2) % 4][:, :],
                                                 ucs[(2 * gic + tt2) % 4][:, :]
                                                 ).then_inc(s_h.h, 1)
                            if e > 0:
                                for fc in range(NFI):
                                    gfc = eg * NFI + fc
                                    for tt2 in range(2):
                                        gw = 2 * gfc + tt2
                                        v.wait_ge(s_w2pe.h, gw + 1)
                                        if fc == 0 and tt2 == 0:
                                            v.wait_ge(s_acc.h, 32 * eg)
                                        sl = acc[:, fc * 1024 + tt2 * 512:
                                                 fc * 1024 + tt2 * 512 + 512]
                                        v.tensor_add(sl, pb(gw % 8),
                                                     sl).then_inc(s_acc.h, 1)

            with ExitStack() as lctx:
                sbn = lambda n, sh, dt: lctx.enter_context(
                    nc.sbuf_tensor(n, sh, dt))
                ypre2 = sbn(f"ypre2_{th}", [128, NFI * 512], F32)
                y1res = sbn(f"y1res_{th}", [128, NFI * 512], F32)
                stat2 = sbn(f"stat2_{th}", [1, 512 * 6], F32)
                ps2 = lctx.enter_context(nc.psum_tensor(f"ps5l_{th}",
                                                        [128, 2048], F32))
                ps_sum2 = lctx.enter_context(nc.psum_tensor(f"ps_sum5_{th}",
                                                            [1, 512], F32))
                ps_sq2 = lctx.enter_context(nc.psum_tensor(f"ps_sq5_{th}",
                                                           [1, 512], F32))
                pb2 = lambda b: ps2[:, b * 512:(b + 1) * 512]
                for qt2 in range(2):
                    m = th * 2 + qt2
                    nrm0 = m * 5
                    with nc.Block() as blk:
                        @blk.sync
                        def _(sy, th=th, qt2=qt2, m=m):
                            src = bass.AP(c.t_y1, th * 1024 + qt2 * 512,
                                          [[TOK, 128], [128 * TOK, NFI],
                                           [1, 512]])
                            sy.dma_start(y1res[:, :],
                                         src).then_inc(s_res.h, 16)

                        @blk.vector
                        def _(v, th=th, qt2=qt2, m=m):
                            v.wait_ge(s_res.h, 16 * (m + 1))
                            mm = None
                            for fc in range(NFI):
                                mm = v.tensor_add(
                                    ypre2[:, fc * 512: fc * 512 + 512],
                                    acc[:, fc * 1024 + qt2 * 512:
                                        fc * 1024 + qt2 * 512 + 512],
                                    y1res[:, fc * 512: fc * 512 + 512])
                            mm.then_inc(s_add.h, 1)

                    with nc.Block() as blk:
                        @blk.scalar
                        def _(s, th=th, qt2=qt2, m=m):
                            mm = None
                            for fc in range(NFI):
                                mm = s.activation(
                                    y1res[:, fc * 512: fc * 512 + 512],
                                    ypre2[:, fc * 512: fc * 512 + 512],
                                    AF.Square)
                            mm.then_inc(s_sq.h, 1)
                            s.wait_ge(s_nrm.h, nrm0 + 5)
                            s.activation(stat2[0:1, 2560:3072],
                                         stat2[0:1, 2048:2560],
                                         AF.Sqrt).then_inc(s_rstd.h, 1)

                        @blk.tensor
                        def _(t, th=th, qt2=qt2, m=m):
                            for fc in range(NFI):
                                t.matmul(ps_sum2[0:1, :], c.ones_col_f[:, :],
                                         ypre2[:, fc * 512: fc * 512 + 512],
                                         start=(fc == 0), stop=(fc == NFI - 1))
                            t.wait_ge(s_sq.h, m + 1)
                            mm = None
                            for fc in range(NFI):
                                mm = t.matmul(ps_sq2[0:1, :],
                                              c.ones_col_f[:, :],
                                              y1res[:, fc * 512: fc * 512 + 512],
                                              start=(fc == 0),
                                              stop=(fc == NFI - 1))
                            mm.then_inc(s_stpe.h, 1)
                            t.wait_ge(s_nrm.h, nrm0 + 1)
                            t.matmul(pb2(0), c.ones_row_f[:, :],
                                     stat2[0:1, 0:512], start=True, stop=True)
                            t.wait_ge(s_rstd.h, m + 1)
                            t.matmul(pb2(1), c.ones_row_f[:, :],
                                     stat2[0:1, 2560:3072], start=True,
                                     stop=True).then_inc(s_bc.h, 1)

                        @blk.vector
                        def _(v, th=th, qt2=qt2, m=m):
                            v.wait_ge(s_stpe.h, m + 1)
                            v.tensor_scalar_mul(stat2[0:1, 0:512],
                                                ps_sum2[0:1, :], INV_D)
                            v.tensor_scalar_mul(stat2[0:1, 512:1024],
                                                ps_sq2[0:1, :],
                                                INV_D).then_inc(s_nrm.h, 1)
                            v.wait_ge(s_nrm.h, nrm0 + 1)
                            v.tensor_mul(stat2[0:1, 1024:1536],
                                         stat2[0:1, 0:512],
                                         stat2[0:1, 0:512]).then_inc(s_nrm.h, 1)
                            v.wait_ge(s_nrm.h, nrm0 + 2)
                            v.tensor_sub(stat2[0:1, 1536:2048],
                                         stat2[0:1, 512:1024],
                                         stat2[0:1, 1024:1536]
                                         ).then_inc(s_nrm.h, 1)
                            v.wait_ge(s_nrm.h, nrm0 + 3)
                            v.tensor_scalar_add(stat2[0:1, 2048:2560],
                                                stat2[0:1, 1536:2048],
                                                EPS).then_inc(s_nrm.h, 1)
                            v.wait_ge(s_nrm.h, nrm0 + 4)
                            v.reciprocal(stat2[0:1, 2048:2560],
                                         stat2[0:1, 2048:2560]
                                         ).then_inc(s_nrm.h, 1)
                            v.wait_ge(s_bc.h, m + 1)
                            mm = None
                            for fc in range(NFI):
                                mm = v.tensor_sub(
                                    ypre2[:, fc * 512: fc * 512 + 512],
                                    ypre2[:, fc * 512: fc * 512 + 512], pb2(0))
                            mm.then_inc(s_sub.h, 1)
                            v.wait_ge(s_sub.h, m + 1)
                            mm = None
                            for fc in range(NFI):
                                mm = v.tensor_mul(
                                    ypre2[:, fc * 512: fc * 512 + 512],
                                    ypre2[:, fc * 512: fc * 512 + 512], pb2(1))
                            mm.then_inc(s_y2.h, 1)

                        @blk.sync
                        def _(sy, th=th, qt2=qt2, m=m):
                            sy.wait_ge(s_y2.h, m + 1)
                            dst = bass.AP(c.t_out, th * 1024 + qt2 * 512,
                                          [[TOK, 128], [128 * TOK, NFI],
                                           [1, 512]])
                            sy.dma_start(dst, ypre2[:, :]).then_inc(s_out.h, 16)
                            sy.wait_ge(s_out.h, 16 * (m + 1))


_NC_CACHE = {}


def kernel(**inputs):
    per_core = _host_prep(inputs)
    key = (LAST_PHASE, tuple(sorted(DBG)))
    if key not in _NC_CACHE:
        _NC_CACHE[key] = build_nc()
    nc = _NC_CACHE[key]
    res = run_bass_kernel_spmd(nc, per_core, core_ids=list(range(8)))
    return _assemble([r["out"] for r in res.results])


if __name__ == "__main__":
    import reference as _ref
    inp = _ref.setup_inputs()
    y = kernel(**{k: np.asarray(v) for k, v in inp.items()})
    ref = np.asarray(_ref.reference(**inp))
    num = float(((y - ref) ** 2).mean())
    den = float((ref ** 2).mean())
    print("resvar", num / den)


# version 02
# speedup vs baseline: 1.0354x; 1.0354x over previous
import numpy as np
import ml_dtypes
from contextlib import ExitStack
import concourse.bass as bass
import concourse.mybir as mybir
from concourse.bass_utils import run_bass_kernel_spmd

F32 = mybir.dt.float32
BF16 = mybir.dt.bfloat16
AF = mybir.ActivationFunctionType
ALU = mybir.AluOpType
AX = mybir.AxisListType

B, S, D = 2, 8192, 2048
H, HD = 16, 128
NFI = D // 128
TOK = 2048
NSLOT = 16
NEXP = 20
NIC = 11
IEXP = 1408
SCALE = float(HD) ** -0.5
EPS = 1e-5
NEG_BIG = -1e30
INV_D = 1.0 / D

LAST_PHASE = "p5"
DBG = ()
P2_LIMIT = None


def _cl(v):
    return max(0, v)


def _host_prep(inp):
    f32 = np.float32
    x = np.asarray(inp["L0_attn_hidden_states"], f32)

    def wimg_sq(w):
        wt = np.ascontiguousarray(np.asarray(w, f32).T)
        return np.ascontiguousarray(
            wt.reshape(NFI, 128, NFI, 128).transpose(1, 0, 2, 3).reshape(128, NFI * D))

    wq_img = wimg_sq(inp["L0_attn_Wq"])
    wk_img = wimg_sq(inp["L0_attn_Wk"])
    wv_img = wimg_sq(inp["L0_attn_Wv"])
    wo_img = wimg_sq(inp["L0_attn_Wo"])

    gs = np.concatenate([np.asarray(inp["L0_ffn_gate_w"], f32),
                         np.asarray(inp["L0_ffn_sgate_w"], f32)], axis=0)
    gs_img = np.ascontiguousarray(
        gs.T.reshape(NFI, 128, 17).transpose(1, 0, 2).reshape(128, NFI * 17))

    w13 = np.asarray(inp["L0_ffn_w13"], f32)
    w2 = np.asarray(inp["L0_ffn_w2"], f32)
    sgu = np.asarray(inp["L0_ffn_shared_gate_up_w"], f32)
    sdw = np.asarray(inp["L0_ffn_shared_down_w"], f32)
    w13_list = [w13[e] for e in range(16)]
    w2_list = [w2[e] for e in range(16)]
    for p in range(4):
        g_p = sgu[p * IEXP:(p + 1) * IEXP]
        u_p = sgu[5632 + p * IEXP: 5632 + (p + 1) * IEXP]
        w13_list.append(np.concatenate([g_p, u_p], axis=0))
        w2_list.append(sdw[:, p * IEXP:(p + 1) * IEXP])

    w13_img = np.empty((NEXP, NIC, 128, 2 * NFI * 128), f32)
    for e in range(NEXP):
        a = w13_list[e].T.reshape(NFI, 128, 2, NIC, 128)
        w13_img[e] = a.transpose(3, 1, 2, 0, 4).reshape(NIC, 128, 2 * NFI * 128)
    w2_img = np.empty((NEXP, NFI, 128, NIC * 128), f32)
    for e in range(NEXP):
        a = w2_list[e].T.reshape(NIC, 128, NFI, 128)
        w2_img[e] = a.transpose(2, 1, 0, 3).reshape(NFI, 128, NIC * 128)

    ident = np.eye(128, dtype=f32)

    per_core = []
    for cid in range(8):
        b, j = cid // 4, cid % 4
        xT = np.ascontiguousarray(x[b].T)
        xseq = xT.reshape(NFI, 128, S)
        own_cols = ((j + 4 * (np.arange(TOK) // 128)) * 128 + np.arange(TOK) % 128)
        xown = np.ascontiguousarray(xT[:, own_cols]).reshape(NFI, 128, TOK)
        p_idx = np.arange(128)[:, None]
        cq = np.arange(4 * 128)[None, :]
        cc, qq = cq // 128, cq % 128
        maskT = np.ascontiguousarray(
            np.where(128 * cc + p_idx <= 128 * j + qq, 0.0, NEG_BIG).astype(f32))
        per_core.append({"xseq": xseq, "xown": xown, "maskT": maskT,
                         "wq": wq_img, "wk": wk_img, "wv": wv_img, "wo": wo_img,
                         "gs": gs_img, "w13": w13_img, "w2": w2_img, "ident": ident})
    return per_core


def _assemble(outs):
    y = np.empty((B, S, D), np.float32)
    for cid in range(8):
        b, j = cid // 4, cid % 4
        o = np.asarray(outs[cid])
        yt = o.reshape(NFI, 128, NSLOT, 128).transpose(2, 3, 0, 1).reshape(TOK, D)
        y[b].reshape(64, 128, D)[j::4] = yt.reshape(NSLOT, 128, D)
    return y


class _C:
    pass


class _Sem:
    __slots__ = ("h",)

    def __init__(self, h):
        self.h = h


def build_nc():
    nc = bass.Bass("TRN2", target_bir_lowering=False)
    c = _C()
    c.nc = nc
    c.t_xseq = nc.dram_tensor("xseq", [NFI, 128, S], F32, kind="ExternalInput")
    c.t_xown = nc.dram_tensor("xown", [NFI, 128, TOK], F32, kind="ExternalInput")
    c.t_mask = nc.dram_tensor("maskT", [128, 512], F32, kind="ExternalInput")
    c.t_wq = nc.dram_tensor("wq", [128, NFI * D], F32, kind="ExternalInput")
    c.t_wk = nc.dram_tensor("wk", [128, NFI * D], F32, kind="ExternalInput")
    c.t_wv = nc.dram_tensor("wv", [128, NFI * D], F32, kind="ExternalInput")
    c.t_wo = nc.dram_tensor("wo", [128, NFI * D], F32, kind="ExternalInput")
    c.t_gs = nc.dram_tensor("gs", [128, NFI * 17], F32, kind="ExternalInput")
    c.t_w13 = nc.dram_tensor("w13", [NEXP, NIC, 128, 2 * NFI * 128], F32,
                             kind="ExternalInput")
    c.t_w2 = nc.dram_tensor("w2", [NEXP, NFI, 128, NIC * 128], F32,
                            kind="ExternalInput")
    c.t_id = nc.dram_tensor("ident", [128, 128], F32, kind="ExternalInput")
    c.t_out = nc.dram_tensor("out", [NFI * 128, TOK], F32, kind="ExternalOutput")

    def scr(name, shape, dt, tag):
        kind = "ExternalOutput" if tag in DBG else "Internal"
        return nc.dram_tensor(name, shape, dt, kind=kind)

    c.t_kscr = scr("k_scr", [H, 128, S], BF16, "kv")
    c.t_vscr = scr("v_scr", [S, D], BF16, "kv")
    c.t_y1 = scr("y1_scr", [NFI, 128, TOK], F32, "y1")
    c.t_y1b = scr("y1b_scr", [NFI, 128, TOK], BF16, "y1b")
    c.t_coef = scr("coef_scr", [NEXP, TOK], BF16, "coef")
    c.t_dbg = {}
    for name, shape, dt in [("q", [128, H * TOK], BF16),
                            ("attn", [128, H * TOK], BF16),
                            ("mhat", [128, H], F32)]:
        if name in DBG:
            c.t_dbg[name] = nc.dram_tensor("dbg_" + name, shape, dt,
                                           kind="ExternalOutput")

    with ExitStack() as top:
        c.top = top
        sb = lambda n, sh, dt: top.enter_context(nc.sbuf_tensor(n, sh, dt))
        c.ones_col_bf = sb("ones_col_bf", [128, 1], BF16)
        c.ones_col_f = sb("ones_col_f", [128, 1], F32)
        c.ones_row_f = sb("ones_row_f", [1, 128], F32)
        c.ones_row_bf = sb("ones_row_bf", [1, 128], BF16)
        c.id_bf = sb("id_bf", [128, 128], BF16)
        c.mask_sb = sb("mask_sb", [128, 512], F32)
        c.mhat_neg = sb("mhat_neg", [128, H], F32)
        c.nrm_k = sb("nrm_k", [1, 256], F32)
        c.nrm_q = sb("nrm_q", [1, 64], F32)
        c.sem_i = 0

        def sem(n):
            c.sem_i += 1
            return _Sem(top.enter_context(nc.semaphore(f"{n}_{c.sem_i}")))
        c.sem = sem

        _p0_consts(c)
        if LAST_PHASE == "p0":
            return nc
        _p1_proj(c, "k")
        if LAST_PHASE == "p1k":
            return nc
        _p1_proj(c, "v")
        if LAST_PHASE == "p1v":
            return nc
        with ExitStack() as attn_scope:
            c.attn_fm = attn_scope.enter_context(
                nc.sbuf_tensor("attn_fm", [128, H * TOK], BF16))
            with ExitStack() as q_scope:
                c.q_fm = q_scope.enter_context(
                    nc.sbuf_tensor("q_fm", [128, H * TOK], BF16))
                _p1_q(c, q_scope)
                if LAST_PHASE == "p1q":
                    return nc
                _p1_mhat(c)
                _dump(c, [("q", c.q_fm), ("mhat", c.mhat_neg)])
                if LAST_PHASE == "p1":
                    return nc
                _p2_attn(c)
            _dump(c, [("attn", c.attn_fm)])
            if LAST_PHASE == "p2":
                return nc
            _p3(c)
        if LAST_PHASE == "p3":
            return nc
        _p5(c)
    return nc


def _dump(c, pairs):
    pairs = [(n, sbuf) for (n, sbuf) in pairs if n in c.t_dbg]
    if not pairs:
        return
    s = c.sem("s_dump")
    with c.nc.Block() as blk:
        @blk.sync
        def _(sy):
            n = 0
            for name, sbuf in pairs:
                sy.dma_start(c.t_dbg[name].ap(), sbuf[:, :]).then_inc(s.h, 16)
                n += 16
            sy.wait_ge(s.h, n)


def _p0_consts(c):
    nc = c.nc
    s = c.sem("s_init")
    with nc.Block() as blk:
        @blk.gpsimd
        def _(g):
            g.memset(c.ones_col_bf[:, :], 1.0)
            g.memset(c.ones_col_f[:, :], 1.0)
            g.memset(c.ones_row_f[:, :], 1.0)
            g.memset(c.ones_row_bf[:, :], 1.0)
            g.dma_start(c.id_bf[:, :], c.t_id.ap()).then_inc(s.h, 16)
            g.dma_start(c.mask_sb[:, :], c.t_mask.ap()).then_inc(s.h, 16)
            g.wait_ge(s.h, 32)


def _p1_proj(c, which):
    nc = c.nc
    NTT = S // 512
    is_k = which == "k"
    t_w = c.t_wk if is_k else c.t_wv
    t_scr = c.t_kscr if is_k else c.t_vscr
    NG = 8 if is_k else 4
    with ExitStack() as ctx:
        sbl = lambda n, sh, dt: ctx.enter_context(nc.sbuf_tensor(n, sh, dt))
        w_half = sbl(f"w_{which}", [128, 8 * NFI * 128], BF16)
        xbuf = [sbl(f"x_{which}{i}", [128, NFI * 512], BF16) for i in range(2)]
        ew = 512 if is_k else 1024
        evac = [sbl(f"ev_{which}{i}", [128, ew], BF16) for i in range(4)]
        ksq = [sbl(f"ksq{i}", [128, 512], BF16) for i in range(2)] if is_k else None
        ps = ctx.enter_context(nc.psum_tensor(f"ps_{which}", [128, 4096], F32))
        pb = lambda b: ps[:, b * 512:(b + 1) * 512]
        s_w = c.sem("sw")
        s_x = c.sem("sx")
        s_pe = c.sem("spe")
        s_ev = c.sem("sev")
        s_npe = c.sem("snpe")
        s_ndve = c.sem("sndve")
        s_st = c.sem("sst")

        def norm_mm(t, i):
            t.wait_ge(s_ev.h, i + 1)
            t.wait_ge(s_ndve.h, _cl(i - 1))
            bank = 6 + (i % 2)
            t.matmul(ps[0:1, bank * 512:(bank + 1) * 512], c.ones_col_bf[:, :],
                     ksq[i % 2][:, :], start=True, stop=True).then_inc(s_npe.h, 1)

        for hf in range(2):
            with nc.Block() as blk:
                @blk.gpsimd
                def _(g, hf=hf):
                    src = bass.AP(t_w, hf * 8 * 128,
                                  [[NFI * D, 128], [D, NFI], [1, 8 * 128]])
                    g.dma_start(w_half[:, :], src).then_inc(s_w.h, 16)
                    for tt in range(NTT):
                        gi = hf * NTT + tt
                        if gi >= 2:
                            g.wait_ge(s_pe.h, (gi - 1) * NG)
                        src = bass.AP(c.t_xseq, tt * 512,
                                      [[S, 128], [128 * S, NFI], [1, 512]])
                        g.dma_start(xbuf[gi % 2][:, :], src).then_inc(s_x.h, 16)

                @blk.tensor
                def _(t, hf=hf):
                    t.wait_ge(s_w.h, 16 * (hf + 1))
                    for tt in range(NTT):
                        gi = hf * NTT + tt
                        t.wait_ge(s_x.h, 16 * (gi + 1))
                        for cg in range(NG):
                            i = gi * NG + cg
                            t.wait_ge(s_ev.h, _cl(i - 1))
                            mm = None
                            if is_k:
                                for fi in range(NFI):
                                    mm = t.matmul(
                                        pb(i % 2),
                                        w_half[:, (fi * 8 + cg) * 128:
                                               (fi * 8 + cg) * 128 + 128],
                                        xbuf[gi % 2][:, fi * 512: fi * 512 + 512],
                                        start=(fi == 0), stop=(fi == NFI - 1))
                            else:
                                for fi in range(NFI):
                                    for b2 in range(2):
                                        mm = t.matmul(
                                            pb(2 * (i % 2) + b2),
                                            xbuf[gi % 2][:, fi * 512 + cg * 128:
                                                         fi * 512 + cg * 128 + 128],
                                            w_half[:, (fi * 8 + 4 * b2) * 128:
                                                   (fi * 8 + 4 * b2) * 128 + 512],
                                            start=(fi == 0), stop=(fi == NFI - 1))
                            mm.then_inc(s_pe.h, 1)
                            if is_k and i >= 1:
                                norm_mm(t, i - 1)
                    if is_k and hf == 1:
                        norm_mm(t, 2 * NTT * 8 - 1)

                @blk.scalar
                def _(s, hf=hf):
                    for tt in range(NTT):
                        gi = hf * NTT + tt
                        for cg in range(NG):
                            i = gi * NG + cg
                            s.wait_ge(s_pe.h, i + 1)
                            s.wait_ge(s_st.h, 16 * _cl(i - 3))
                            if is_k:
                                s.wait_ge(s_npe.h, _cl(i - 1))
                                s.copy(evac[i % 4][:, :], pb(i % 2))
                                s.activation(ksq[i % 2][:, :], pb(i % 2),
                                             AF.Square).then_inc(s_ev.h, 1)
                            else:
                                s.copy(evac[i % 4][:, 0:512], pb(2 * (i % 2)))
                                s.copy(evac[i % 4][:, 512:1024],
                                       pb(2 * (i % 2) + 1)).then_inc(s_ev.h, 1)

                if is_k:
                    @blk.vector
                    def _(v, hf=hf):
                        lo = 0 if hf == 0 else 127
                        hi = 127 if hf == 0 else 256
                        for i in range(lo, hi):
                            v.wait_ge(s_npe.h, i + 1)
                            bank = 6 + (i % 2)
                            gi2, cg2 = i // 8, i % 8
                            hf2, tt2 = gi2 // NTT, gi2 % NTT
                            slot = (hf2 * 8 + cg2) * 16 + tt2
                            v.reduce_max(c.nrm_k[:, slot:slot + 1],
                                         ps[0:1, bank * 512:(bank + 1) * 512],
                                         axis=AX.X).then_inc(s_ndve.h, 1)

                @blk.sync
                def _(sy, hf=hf):
                    for tt in range(NTT):
                        gi = hf * NTT + tt
                        for cg in range(NG):
                            i = gi * NG + cg
                            sy.wait_ge(s_ev.h, i + 1)
                            if is_k:
                                fc = hf * 8 + cg
                                dst = bass.AP(t_scr, fc * 128 * S + tt * 512,
                                              [[S, 128], [1, 512]])
                            else:
                                row0 = tt * 512 + cg * 128
                                dst = bass.AP(t_scr, row0 * D + hf * 1024,
                                              [[D, 128], [1, 1024]])
                            sy.dma_start(dst, evac[i % 4][:, :]).then_inc(s_st.h, 16)
                    if hf == 1:
                        sy.wait_ge(s_st.h, 16 * 2 * NTT * NG)


def _p1_q(c, scope):
    nc = c.nc
    NTT = 4
    with ExitStack() as ctx:
        sbl = lambda n, sh, dt: ctx.enter_context(nc.sbuf_tensor(n, sh, dt))
        w_half = sbl("w_q", [128, 8 * NFI * 128], BF16)
        xbuf = [sbl(f"x_q{i}", [128, NFI * 512], BF16) for i in range(2)]
        qsq = [sbl(f"qsq{i}", [128, 512], BF16) for i in range(2)]
        ps = ctx.enter_context(nc.psum_tensor("ps_q", [128, 4096], F32))
        pb = lambda b: ps[:, b * 512:(b + 1) * 512]
        s_w = c.sem("sqw")
        s_x = c.sem("sqx")
        s_pe = c.sem("sqpe")
        s_ev = c.sem("sqev")
        s_npe = c.sem("sqnpe")
        s_ndve = c.sem("sqndve")

        def norm_mm(t, i):
            t.wait_ge(s_ev.h, i + 1)
            t.wait_ge(s_ndve.h, _cl(i - 1))
            bank = 6 + (i % 2)
            t.matmul(ps[0:1, bank * 512:(bank + 1) * 512], c.ones_col_bf[:, :],
                     qsq[i % 2][:, :], start=True, stop=True).then_inc(s_npe.h, 1)

        for hf in range(2):
            with nc.Block() as blk:
                @blk.gpsimd
                def _(g, hf=hf):
                    src = bass.AP(c.t_wq, hf * 8 * 128,
                                  [[NFI * D, 128], [D, NFI], [1, 8 * 128]])
                    g.dma_start(w_half[:, :], src).then_inc(s_w.h, 16)
                    for tt in range(NTT):
                        gi = hf * NTT + tt
                        if gi >= 2:
                            g.wait_ge(s_pe.h, (gi - 1) * 8)
                        src = bass.AP(c.t_xown, tt * 512,
                                      [[TOK, 128], [128 * TOK, NFI], [1, 512]])
                        g.dma_start(xbuf[gi % 2][:, :], src).then_inc(s_x.h, 16)

                @blk.tensor
                def _(t, hf=hf):
                    t.wait_ge(s_w.h, 16 * (hf + 1))
                    for tt in range(NTT):
                        gi = hf * NTT + tt
                        t.wait_ge(s_x.h, 16 * (gi + 1))
                        for cg in range(8):
                            i = gi * 8 + cg
                            t.wait_ge(s_ev.h, _cl(i - 1))
                            mm = None
                            for fi in range(NFI):
                                mm = t.matmul(
                                    pb(i % 2),
                                    w_half[:, (fi * 8 + cg) * 128:
                                           (fi * 8 + cg) * 128 + 128],
                                    xbuf[gi % 2][:, fi * 512: fi * 512 + 512],
                                    start=(fi == 0), stop=(fi == NFI - 1))
                            mm.then_inc(s_pe.h, 1)
                            if i >= 1:
                                norm_mm(t, i - 1)
                    if hf == 1:
                        norm_mm(t, 2 * NTT * 8 - 1)

                @blk.scalar
                def _(s, hf=hf):
                    for tt in range(NTT):
                        gi = hf * NTT + tt
                        for cg in range(8):
                            i = gi * 8 + cg
                            fc = hf * 8 + cg
                            s.wait_ge(s_pe.h, i + 1)
                            s.wait_ge(s_npe.h, _cl(i - 1))
                            s.activation(
                                c.q_fm[:, fc * TOK + tt * 512:
                                       fc * TOK + tt * 512 + 512],
                                pb(i % 2), AF.Copy, scale=SCALE)
                            s.activation(qsq[i % 2][:, :], pb(i % 2),
                                         AF.Square, scale=SCALE).then_inc(s_ev.h, 1)

                @blk.vector
                def _(v, hf=hf):
                    lo = 0 if hf == 0 else 31
                    hi = 31 if hf == 0 else 64
                    for i in range(lo, hi):
                        v.wait_ge(s_npe.h, i + 1)
                        bank = 6 + (i % 2)
                        gi2, cg2 = i // 8, i % 8
                        hf2, tt2 = gi2 // NTT, gi2 % NTT
                        slot = (hf2 * 8 + cg2) * 4 + tt2
                        v.reduce_max(c.nrm_q[:, slot:slot + 1],
                                     ps[0:1, bank * 512:(bank + 1) * 512],
                                     axis=AX.X).then_inc(s_ndve.h, 1)


def _p1_mhat(c):
    nc = c.nc
    with ExitStack() as ctx:
        sbl = lambda n, sh, dt: ctx.enter_context(nc.sbuf_tensor(n, sh, dt))
        km = sbl("km2", [1, H], F32)
        qm = sbl("qm2", [1, H], F32)
        m2 = sbl("m2", [1, H], F32)
        mh = sbl("mh", [1, H], F32)
        mhn = sbl("mhn", [1, H], F32)
        ps = ctx.enter_context(nc.psum_tensor("ps_mh", [128, 512], F32))
        s_a = c.sem("sma")
        s_b = c.sem("smb")
        s_d = c.sem("smd")
        with nc.Block() as blk:
            @blk.vector
            def _(v):
                mm = None
                for fc in range(H):
                    v.reduce_max(km[:, fc:fc + 1],
                                 c.nrm_k[0:1, fc * 16: fc * 16 + 16], axis=AX.X)
                    mm = v.reduce_max(qm[:, fc:fc + 1],
                                      c.nrm_q[0:1, fc * 4: fc * 4 + 4],
                                      axis=AX.X)
                mm.then_inc(s_a.h, 1)
                v.wait_ge(s_a.h, 1)
                v.tensor_mul(m2[:, :], km[:, :], qm[:, :]).then_inc(s_a.h, 1)

            @blk.scalar
            def _(s):
                s.wait_ge(s_a.h, 2)
                s.activation(mh[:, :], m2[:, :], AF.Sqrt).then_inc(s_b.h, 1)
                s.wait_ge(s_b.h, 1)
                s.activation(mhn[:, :], mh[:, :], AF.Copy,
                             scale=-1.0).then_inc(s_b.h, 2)

            @blk.tensor
            def _(t):
                t.wait_ge(s_b.h, 3)
                t.matmul(ps[:, 0:H], c.ones_row_f[:, :], mhn[:, :],
                         start=True, stop=True).then_inc(s_d.h, 1)
        with nc.Block() as blk2:
            @blk2.scalar
            def _(s):
                s.copy(c.mhat_neg[:, :], ps[:, 0:H])


def _p2_schedule():
    tiles, ktloads, ghps = [], [], []
    m = kl = 0
    for g in range(4):
        for hp in range(8):
            ghp = {"g": g, "hp": hp, "idx": len(ghps), "m0": m, "tiles": []}
            for kt in range(4 * g + 4):
                rec = {"g": g, "hp": hp, "kt": kt, "idx": kl, "m0": m}
                for h2 in range(2):
                    for cc in range(4):
                        t = {"idx": m, "g": g, "hp": hp, "kt": kt, "h2": h2,
                             "cc": cc, "kl": kl,
                             "qoff": _cl(kt - 4 * g) * 128,
                             "diag": 4 * g <= kt <= 4 * g + 3,
                             "first": kt == 0 and cc == 0,
                             "last": kt == 4 * g + 3 and cc == 3}
                        tiles.append(t)
                        ghp["tiles"].append(t)
                        m += 1
                rec["m1"] = m
                ktloads.append(rec)
                kl += 1
            ghp["m1"] = m
            ghps.append(ghp)
    nd = 0
    for t in tiles:
        if t["diag"]:
            nd += 1
        t["ndiag"] = nd
    return tiles, ktloads, ghps


def _p2_attn(c):
    nc = c.nc
    tiles, ktloads, ghps = _p2_schedule()
    if P2_LIMIT is not None:
        ghps = ghps[:P2_LIMIT]
        tiles = [t for g in ghps for t in g["tiles"]]
        kls = {t["kl"] for t in tiles}
        ktloads = [r for r in ktloads if r["idx"] in kls]
    with ExitStack() as ctx:
        sbl = lambda n, sh, dt: ctx.enter_context(nc.sbuf_tensor(n, sh, dt))
        kbuf = [sbl(f"kbuf{i}", [128, 1024], BF16) for i in range(2)]
        vbuf = [sbl(f"vbuf{i}", [128, 1024], BF16) for i in range(2)]
        ptb = [sbl(f"ptb{i}", [128, 512], BF16) for i in range(4)]
        linv = sbl("linv", [1, 1024], F32)
        linv_rep = sbl("linv_rep", [128, 1024], F32)
        ps = ctx.enter_context(nc.psum_tensor("ps_at", [128, 4096], F32))
        s_kv = c.sem("skv")
        s_spe = c.sem("sspe")
        s_mask = c.sem("smask")
        s_exp = c.sem("sexp")
        s_av = c.sem("sav")
        s_rec = c.sem("srec")
        s_bc = c.sem("sbc")
        s_lr = c.sem("slr")
        s_fin = c.sem("sfin")

        st_ps = lambda m: ps[:, (m % 4) * 512:(m % 4) * 512 + 512]
        av_ps = lambda h2: ps[:, (4 + h2) * 512:(4 + h2) * 512 + 512]
        l_ps = lambda h2: ps[0:1, (6 + h2) * 512:(6 + h2) * 512 + 512]
        bc_ps = lambda h2: ps[:, (2 + h2) * 512:(2 + h2) * 512 + 512]

        def lav(t, tl):
            m, h2, cc, qo = tl["idx"], tl["h2"], tl["cc"], tl["qoff"]
            t.wait_ge(s_exp.h, m + 1)
            t.matmul(l_ps(h2)[0:1, qo:512], c.ones_col_bf[:, :],
                     ptb[m % 4][:, qo:512],
                     start=tl["first"], stop=tl["last"])
            t.matmul(av_ps(h2)[:, qo:512],
                     vbuf[tl["kl"] % 2][:, cc * 256 + h2 * 128:
                                        cc * 256 + h2 * 128 + 128],
                     ptb[m % 4][:, qo:512],
                     start=tl["first"], stop=tl["last"]).then_inc(s_av.h, 1)

        with nc.Block() as blk:
            @blk.sync
            def _(sy):
                for rec in ktloads:
                    i = rec["idx"]
                    if i >= 2:
                        sy.wait_ge(s_av.h, ktloads[i - 2]["m1"])
                    hp, kt = rec["hp"], rec["kt"]
                    srck = bass.AP(c.t_kscr, (2 * hp) * 128 * S + kt * 512,
                                   [[S, 128], [128 * S, 2], [1, 512]])
                    sy.dma_start(kbuf[i % 2][:, :], srck).then_inc(s_kv.h, 16)
                    srcv = bass.AP(c.t_vscr, (kt * 512) * D + hp * 256,
                                   [[D, 128], [128 * D, 4], [1, 256]])
                    sy.dma_start(vbuf[i % 2][:, :], srcv).then_inc(s_kv.h, 16)

            @blk.tensor
            def _(t):
                for ghp in ghps:
                    g, hp = ghp["g"], ghp["hp"]
                    for tl in ghp["tiles"]:
                        m = tl["idx"]
                        if m == ghp["m0"]:
                            t.wait_ge(s_rec.h, 2 * ghp["idx"])
                            t.wait_ge(s_fin.h, 2 * ghp["idx"])
                        t.wait_ge(s_kv.h, 32 * (tl["kl"] + 1))
                        t.wait_ge(s_exp.h, _cl(m - 3))
                        hg = 2 * hp + tl["h2"]
                        qo = tl["qoff"]
                        t.matmul(st_ps(m)[:, qo:512],
                                 kbuf[tl["kl"] % 2][:, tl["h2"] * 512 +
                                                    tl["cc"] * 128:
                                                    tl["h2"] * 512 +
                                                    tl["cc"] * 128 + 128],
                                 c.q_fm[:, hg * TOK + 4 * g * 128 + qo:
                                        hg * TOK + 4 * g * 128 + 512],
                                 start=True, stop=True).then_inc(s_spe.h, 1)
                        if m > ghp["m0"]:
                            lav(t, tiles[m - 1])
                    lav(t, tiles[ghp["m1"] - 1])
                    for h2 in range(2):
                        t.wait_ge(s_rec.h, 2 * ghp["idx"] + h2 + 1)
                        t.wait_ge(s_lr.h, 2 * ghp["idx"])
                        t.matmul(bc_ps(h2), c.ones_row_f[:, :],
                                 linv[0:1, h2 * 512: h2 * 512 + 512],
                                 start=True, stop=True).then_inc(s_bc.h, 1)

            @blk.vector
            def _(v):
                for ghp in ghps:
                    g = ghp["g"]
                    for tl in ghp["tiles"]:
                        if not tl["diag"]:
                            continue
                        m, cc, qo = tl["idx"], tl["cc"], tl["qoff"]
                        v.wait_ge(s_spe.h, m + 1)
                        v.tensor_add(st_ps(m)[:, qo: qo + 128],
                                     st_ps(m)[:, qo: qo + 128],
                                     c.mask_sb[:, cc * 128: cc * 128 + 128]
                                     ).then_inc(s_mask.h, 1)
                    v.wait_ge(s_av.h, ghp["m1"])
                    for h2 in range(2):
                        v.reciprocal(linv[0:1, h2 * 512: h2 * 512 + 512],
                                     l_ps(h2)).then_inc(s_rec.h, 1)
                    for h2 in range(2):
                        v.wait_ge(s_lr.h, 2 * ghp["idx"] + h2 + 1)
                        hg = 2 * ghp["hp"] + h2
                        v.tensor_mul(
                            c.attn_fm[:, hg * TOK + 4 * g * 128:
                                      hg * TOK + 4 * g * 128 + 512],
                            av_ps(h2),
                            linv_rep[:, h2 * 512: h2 * 512 + 512]
                            ).then_inc(s_fin.h, 1)

            @blk.scalar
            def _(s):
                for ghp in ghps:
                    for tl in ghp["tiles"]:
                        m = tl["idx"]
                        hg = 2 * tl["hp"] + tl["h2"]
                        if tl["diag"]:
                            s.wait_ge(s_mask.h, tl["ndiag"])
                        else:
                            s.wait_ge(s_spe.h, m + 1)
                        s.wait_ge(s_av.h, _cl(m - 3))
                        qo = tl["qoff"]
                        s.activation(ptb[m % 4][:, qo:512],
                                     st_ps(m)[:, qo:512], AF.Exp,
                                     bias=c.mhat_neg[:, hg:hg + 1]
                                     ).then_inc(s_exp.h, 1)
                    for h2 in range(2):
                        s.wait_ge(s_bc.h, 2 * ghp["idx"] + h2 + 1)
                        s.copy(linv_rep[:, h2 * 512: h2 * 512 + 512],
                               bc_ps(h2)).then_inc(s_lr.h, 1)


def _p3(c):
    nc = c.nc
    with ExitStack() as ctx:
        sbl = lambda n, sh, dt: ctx.enter_context(nc.sbuf_tensor(n, sh, dt))
        wobuf = [sbl(f"wob{i}", [128, NFI * 128], BF16) for i in range(4)]
        gs_sb = sbl("gs_sb", [128, NFI * 17], F32)
        ypre = sbl("ypre3", [128, NFI * 512], F32)
        y1f = sbl("y1f3", [128, NFI * 512], F32)
        y1b = sbl("y1b3", [128, NFI * 512], BF16)
        stat = sbl("stat3", [1, 512 * 6], F32)
        probs = [sbl(f"probs3{i}", [128, 16], F32) for i in range(4)]
        prn = [sbl(f"prn3{i}", [128, 16], F32) for i in range(4)]
        wk4 = [[sbl(f"wk43{i}_{j}", [128, 16], F32) for j in range(2)]
               for i in range(4)]
        mx = [sbl(f"mx3{i}", [128, 1], F32) for i in range(4)]
        nmx = [sbl(f"nmx3{i}", [128, 1], F32) for i in range(4)]
        eq = [sbl(f"eq3{i}", [128, 16], F32) for i in range(4)]
        coef_f = [sbl(f"coef_f3{i}", [128, 16], F32) for i in range(4)]
        coef_bf = [sbl(f"coef_bf3{i}", [128, 17], BF16) for i in range(4)]
        crow_st = [sbl(f"crow3{i}", [1, NEXP * 128], BF16) for i in range(4)]
        sgc = [sbl(f"sgc3{i}", [128, 1], F32) for i in range(4)]
        lsum = [sbl(f"lsum3{i}", [128, 1], F32) for i in range(4)]
        lin = [sbl(f"lin3{i}", [128, 1], F32) for i in range(4)]
        ps = ctx.enter_context(nc.psum_tensor("ps_p3", [128, 2048], F32))
        ps_sum = ctx.enter_context(nc.psum_tensor("ps_sum3", [1, 512], F32))
        ps_sq = ctx.enter_context(nc.psum_tensor("ps_sq3", [1, 512], F32))
        ps_rt = ctx.enter_context(nc.psum_tensor("ps_rt3", [128, 68], F32))
        pb = lambda b: ps[:, b * 512:(b + 1) * 512]
        ps_tr = ps[0:1, 0:128]

        s_w = c.sem("s3w")
        s_gs = c.sem("s3gs")
        s_x = c.sem("s3x")
        s_ope = c.sem("s3ope")
        s_yp = c.sem("s3yp")
        s_sq = c.sem("s3sq")
        s_stpe = c.sem("s3stpe")
        s_nrm = c.sem("s3nrm")
        s_rstd = c.sem("s3rstd")
        s_bc = c.sem("s3bc")
        s_sub = c.sem("s3sub")
        s_y1 = c.sem("s3y1")
        s_y1b = c.sem("s3y1b")
        s_st = c.sem("s3st")
        s_rpe = c.sem("s3rpe")
        s_smc = [c.sem(f"s3smc{i}") for i in range(4)]
        s_cst = c.sem("s3cst")
        c.sm_cnt = [0, 0, 0, 0]

        with nc.Block() as blk:
            @blk.gpsimd
            def _(g):
                g.dma_start(gs_sb[:, :], c.t_gs.ap()).then_inc(s_gs.h, 16)
                g.wait_ge(s_gs.h, 16)

        for qt in range(4):
            with nc.Block() as blk:
                @blk.gpsimd
                def _(g, qt=qt):
                    for fc in range(NFI):
                        i = qt * NFI + fc
                        if i >= 4:
                            g.wait_ge(s_ope.h, i - 3)
                        src = bass.AP(c.t_wo, fc * 128,
                                      [[NFI * D, 128], [D, NFI], [1, 128]])
                        g.dma_start(wobuf[fc % 4][:, :], src).then_inc(s_w.h, 16)

                @blk.sync
                def _(sy, qt=qt):
                    src = bass.AP(c.t_xown, qt * 512,
                                  [[TOK, 128], [128 * TOK, NFI], [1, 512]])
                    sy.dma_start(ypre[:, :], src).then_inc(s_x.h, 16)

                @blk.tensor
                def _(t, qt=qt):
                    t.wait_ge(s_x.h, 16 * (qt + 1))
                    for fc in range(NFI):
                        i = qt * NFI + fc
                        t.wait_ge(s_w.h, 16 * (i + 1))
                        t.wait_ge(s_yp.h, _cl(i - 1))
                        mm = None
                        for fi in range(NFI):
                            mm = t.matmul(
                                pb(i % 2),
                                wobuf[fc % 4][:, fi * 128: fi * 128 + 128],
                                c.attn_fm[:, fi * TOK + qt * 512:
                                          fi * TOK + qt * 512 + 512],
                                start=(fi == 0), stop=(fi == NFI - 1))
                        mm.then_inc(s_ope.h, 1)

                @blk.vector
                def _(v, qt=qt):
                    for fc in range(NFI):
                        i = qt * NFI + fc
                        v.wait_ge(s_ope.h, i + 1)
                        v.tensor_add(ypre[:, fc * 512: fc * 512 + 512],
                                     pb(i % 2),
                                     ypre[:, fc * 512: fc * 512 + 512]
                                     ).then_inc(s_yp.h, 1)

            nrm0 = qt * 5
            with nc.Block() as blk:
                @blk.scalar
                def _(s, qt=qt):
                    mm = None
                    for fc in range(NFI):
                        mm = s.activation(y1f[:, fc * 512: fc * 512 + 512],
                                          ypre[:, fc * 512: fc * 512 + 512],
                                          AF.Square)
                    mm.then_inc(s_sq.h, 1)
                    s.wait_ge(s_nrm.h, nrm0 + 5)
                    s.activation(stat[0:1, 2560:3072], stat[0:1, 2048:2560],
                                 AF.Sqrt).then_inc(s_rstd.h, 1)
                    s.wait_ge(s_y1.h, qt + 1)
                    mm = None
                    for fc in range(NFI):
                        mm = s.copy(y1b[:, fc * 512: fc * 512 + 512],
                                    y1f[:, fc * 512: fc * 512 + 512])
                    mm.then_inc(s_y1b.h, 1)

                @blk.tensor
                def _(t, qt=qt):
                    for fc in range(NFI):
                        t.matmul(ps_sum[0:1, :], c.ones_col_f[:, :],
                                 ypre[:, fc * 512: fc * 512 + 512],
                                 start=(fc == 0), stop=(fc == NFI - 1))
                    t.wait_ge(s_sq.h, qt + 1)
                    mm = None
                    for fc in range(NFI):
                        mm = t.matmul(ps_sq[0:1, :], c.ones_col_f[:, :],
                                      y1f[:, fc * 512: fc * 512 + 512],
                                      start=(fc == 0), stop=(fc == NFI - 1))
                    mm.then_inc(s_stpe.h, 1)
                    t.wait_ge(s_nrm.h, nrm0 + 1)
                    t.matmul(pb(2), c.ones_row_f[:, :], stat[0:1, 0:512],
                             start=True, stop=True)
                    t.wait_ge(s_rstd.h, qt + 1)
                    t.matmul(pb(3), c.ones_row_f[:, :], stat[0:1, 2560:3072],
                             start=True, stop=True).then_inc(s_bc.h, 1)
                    t.wait_ge(s_y1.h, qt + 1)
                    mm = None
                    for fc in range(NFI):
                        for tc in range(4):
                            mm = t.matmul(ps_rt[:, tc * 17: tc * 17 + 17],
                                          y1f[:, fc * 512 + tc * 128:
                                              fc * 512 + tc * 128 + 128],
                                          gs_sb[:, fc * 17: fc * 17 + 17],
                                          start=(fc == 0 and tc == 0),
                                          stop=(fc == NFI - 1))
                    mm.then_inc(s_rpe.h, 1)

                @blk.vector
                def _(v, qt=qt):
                    v.wait_ge(s_stpe.h, qt + 1)
                    v.tensor_scalar_mul(stat[0:1, 0:512], ps_sum[0:1, :], INV_D)
                    v.tensor_scalar_mul(stat[0:1, 512:1024], ps_sq[0:1, :],
                                        INV_D).then_inc(s_nrm.h, 1)
                    v.wait_ge(s_nrm.h, nrm0 + 1)
                    v.tensor_mul(stat[0:1, 1024:1536], stat[0:1, 0:512],
                                 stat[0:1, 0:512]).then_inc(s_nrm.h, 1)
                    v.wait_ge(s_nrm.h, nrm0 + 2)
                    v.tensor_sub(stat[0:1, 1536:2048], stat[0:1, 512:1024],
                                 stat[0:1, 1024:1536]).then_inc(s_nrm.h, 1)
                    v.wait_ge(s_nrm.h, nrm0 + 3)
                    v.tensor_scalar_add(stat[0:1, 2048:2560],
                                        stat[0:1, 1536:2048],
                                        EPS).then_inc(s_nrm.h, 1)
                    v.wait_ge(s_nrm.h, nrm0 + 4)
                    v.reciprocal(stat[0:1, 2048:2560],
                                 stat[0:1, 2048:2560]).then_inc(s_nrm.h, 1)
                    v.wait_ge(s_bc.h, qt + 1)
                    mm = None
                    for fc in range(NFI):
                        mm = v.tensor_sub(y1f[:, fc * 512: fc * 512 + 512],
                                          ypre[:, fc * 512: fc * 512 + 512],
                                          pb(2))
                    mm.then_inc(s_sub.h, 1)
                    v.wait_ge(s_sub.h, qt + 1)
                    mm = None
                    for fc in range(NFI):
                        mm = v.tensor_mul(y1f[:, fc * 512: fc * 512 + 512],
                                          y1f[:, fc * 512: fc * 512 + 512],
                                          pb(3))
                    mm.then_inc(s_y1.h, 1)

                @blk.sync
                def _(sy, qt=qt):
                    sy.wait_ge(s_y1.h, qt + 1)
                    dst = bass.AP(c.t_y1, qt * 512,
                                  [[TOK, 128], [128 * TOK, NFI], [1, 512]])
                    sy.dma_start(dst, y1f[:, :]).then_inc(s_st.h, 16)
                    sy.wait_ge(s_y1b.h, qt + 1)
                    dstb = bass.AP(c.t_y1b, qt * 512,
                                   [[TOK, 128], [128 * TOK, NFI], [1, 512]])
                    sy.dma_start(dstb, y1b[:, :]).then_inc(s_st.h, 16)
                    sy.wait_ge(s_st.h, 32 * (qt + 1))

            chains = [[] for _ in range(4)]

            for tc in range(4):
                ch = chains[tc]
                V = lambda fn, ch=ch: ch.append(("v", fn))
                A = lambda fn, ch=ch: ch.append(("s", fn))
                T = lambda fn, ch=ch: ch.append(("t", fn))
                r16 = ps_rt[:, tc * 17: tc * 17 + 16]
                rsg = ps_rt[:, tc * 17 + 16: tc * 17 + 17]
                ps_tr = ps[0:1, tc * 512: tc * 512 + 128]
                V(lambda v, r16=r16, tc=tc: v.reduce_max(
                    nmx[tc][:, :], r16, axis=AX.X, negate=True))
                A(lambda s, r16=r16, tc=tc: s.activation(
                    probs[tc][:, :], r16, AF.Exp, bias=nmx[tc][:, :],
                    accum_out=lsum[tc][:, :]))
                A(lambda s, rsg=rsg, tc=tc: s.activation(
                    sgc[tc][:, :], rsg, AF.Sigmoid))
                V(lambda v, tc=tc: v.reciprocal(lin[tc][:, :], lsum[tc][:, :]))
                V(lambda v, tc=tc: v.tensor_scalar_mul(
                    prn[tc][:, :], probs[tc][:, :], lin[tc][:, :]))
                for it in range(4):
                    src = prn[tc] if it == 0 else wk4[tc][it % 2]
                    V(lambda v, src=src, tc=tc: v.reduce_max(
                        mx[tc][:, :], src[:, :], axis=AX.X))
                    if it < 3:
                        V(lambda v, src=src, tc=tc: v.tensor_scalar(
                            eq[tc][:, :], src[:, :], mx[tc][:, :], None,
                            ALU.is_ge))
                        V(lambda v, src=src, it=it, tc=tc:
                          v.scalar_tensor_tensor(
                              wk4[tc][(it + 1) % 2][:, :], eq[tc][:, :], -1e9,
                              src[:, :], ALU.mult, ALU.add))
                V(lambda v, tc=tc: v.tensor_scalar(
                    eq[tc][:, :], prn[tc][:, :], mx[tc][:, :], None, ALU.is_ge))
                V(lambda v, tc=tc: v.tensor_mul(
                    coef_f[tc][:, :], prn[tc][:, :], eq[tc][:, :]))
                V(lambda v, tc=tc: v.tensor_copy(
                    coef_bf[tc][:, 0:16], coef_f[tc][:, :]))
                V(lambda v, tc=tc: v.tensor_copy(
                    coef_bf[tc][:, 16:17], sgc[tc][:, :]))
                firstcopy = [True]
                for e in range(17):
                    col = e if e < 16 else 16
                    T(lambda t, col=col, tc=tc, ps_tr=ps_tr: t.matmul(
                        ps_tr, coef_bf[tc][:, col:col + 1], c.id_bf[:, :],
                        start=True, stop=True))
                    dsts = [e] if e < 16 else [16, 17, 18, 19]
                    for de in dsts:
                        fc_ = firstcopy[0]
                        firstcopy[0] = False

                        def cp(s, de=de, tc=tc, fc_=fc_, ps_tr=ps_tr, qt=qt):
                            if fc_:
                                s.wait_ge(s_cst.h, 16 * _cl(qt * 4 + tc - 3))
                            return s.copy(
                                crow_st[tc][0:1, de * 128: de * 128 + 128],
                                ps_tr)
                        A(cp)

            base = [c.sm_cnt[tc] for tc in range(4)]
            order = []
            mx_len = max(len(chn) for chn in chains)
            for pos in range(mx_len):
                for tc in range(4):
                    if pos < len(chains[tc]):
                        order.append((tc, pos))

            with nc.Block() as blk:
                def run(eng_code):
                    def emit(eng):
                        for tc, pos in order:
                            e2, fn = chains[tc][pos]
                            if e2 == eng_code:
                                eng.wait_ge(s_smc[tc].h, base[tc] + pos)
                                fn(eng).then_inc(s_smc[tc].h, 1)
                    return emit

                blk.vector(run("v"))
                blk.scalar(run("s"))
                blk.tensor(run("t"))

                @blk.sync
                def _(sy, qt=qt, base=base):
                    for tc in range(4):
                        sy.wait_ge(s_smc[tc].h, base[tc] + len(chains[tc]))
                        dst = bass.AP(c.t_coef, qt * 512 + tc * 128,
                                      [[TOK, NEXP], [1, 128]])
                        sy.dma_start(dst, crow_st[tc][0:1, :]
                                     ).then_inc(s_cst.h, 16)
                    sy.wait_ge(s_cst.h, 16 * 4 * (qt + 1))
            for tc in range(4):
                c.sm_cnt[tc] += len(chains[tc])


def _p5(c):
    nc = c.nc
    s_y1bh = c.sem("s5y")
    s_w13 = c.sem("s5w13")
    s_gupe = c.sem("s5gupe")
    s_silu = c.sem("s5silu")
    s_uc = c.sem("s5uc")
    s_h = c.sem("s5h")
    s_cpe = c.sem("s5cpe")
    s_crep = c.sem("s5crep")
    s_w2 = c.sem("s5w2")
    s_w2pe = c.sem("s5w2pe")
    s_acc = c.sem("s5acc")
    s_crow = c.sem("s5crow")
    s_res = c.sem("s5res")
    s_add = c.sem("s5add")
    s_sq = c.sem("s5sq")
    s_stpe = c.sem("s5stpe")
    s_nrm = c.sem("s5nrm")
    s_rstd = c.sem("s5rstd")
    s_bc = c.sem("s5bc")
    s_sub = c.sem("s5sub")
    s_y2 = c.sem("s5y2")
    s_out = c.sem("s5out")

    for th in range(2):
        with ExitStack() as thc:
            sbl = lambda n, sh, dt: thc.enter_context(nc.sbuf_tensor(n, sh, dt))
            y1bh = sbl(f"y1bh{th}", [128, NFI * 1024], BF16)
            acc = sbl(f"acc5{th}", [128, NFI * 1024], F32)
            with ExitStack() as ectx:
                sbe = lambda n, sh, dt: ectx.enter_context(
                    nc.sbuf_tensor(n, sh, dt))
                w13buf = [sbe(f"w13b{th}_{i}", [128, 2 * NFI * 128], BF16)
                          for i in range(2)]
                w2buf = [sbe(f"w2b{th}_{i}", [128, NIC * 128], BF16)
                         for i in range(2)]
                h_sb = sbe(f"h5_{th}", [128, NIC * 1024], BF16)
                sgs = [sbe(f"sg5{th}_{i}", [128, 512], BF16) for i in range(4)]
                ucs = [sbe(f"uc5{th}_{i}", [128, 512], BF16) for i in range(4)]
                crep = sbe(f"crep5_{th}", [128, 1024], BF16)
                crow = [sbe(f"crow5_{th}_{i}", [1, 1024], BF16)
                        for i in range(2)]
                ps = ectx.enter_context(nc.psum_tensor(f"ps5_{th}",
                                                       [128, 4096], F32))
                pb = lambda b: ps[:, b * 512:(b + 1) * 512]

                with nc.Block() as blk:
                    @blk.sync
                    def _(sy, th=th):
                        src = bass.AP(c.t_y1b, th * 1024,
                                      [[TOK, 128], [128 * TOK, NFI], [1, 1024]])
                        sy.dma_start(y1bh[:, :], src).then_inc(s_y1bh.h, 16)
                        for e in range(NEXP):
                            eg = th * NEXP + e
                            if eg >= 2:
                                sy.wait_ge(s_cpe.h, 2 * (eg - 1))
                            src = bass.AP(c.t_coef, e * TOK + th * 1024,
                                          [[1, 1], [1, 1024]])
                            sy.dma_start(crow[e % 2][0:1, :],
                                         src).then_inc(s_crow.h, 16)

                    @blk.gpsimd
                    def _(g, th=th):
                        for e in range(NEXP):
                            eg = th * NEXP + e
                            for ic in range(NIC):
                                gic = eg * NIC + ic
                                if gic >= 2:
                                    g.wait_ge(s_gupe.h, gic - 1)
                                src = bass.AP(
                                    c.t_w13,
                                    (e * NIC + ic) * 128 * 2 * NFI * 128,
                                    [[2 * NFI * 128, 128], [1, 2 * NFI * 128]])
                                g.dma_start(w13buf[ic % 2][:, :],
                                            src).then_inc(s_w13.h, 16)
                            for fc in range(NFI):
                                gfc = eg * NFI + fc
                                if gfc >= 2:
                                    g.wait_ge(s_w2pe.h, 2 * (gfc - 1))
                                src = bass.AP(
                                    c.t_w2,
                                    (e * NFI + fc) * 128 * NIC * 128,
                                    [[NIC * 128, 128], [1, NIC * 128]])
                                g.dma_start(w2buf[fc % 2][:, :],
                                            src).then_inc(s_w2.h, 16)

                    @blk.tensor
                    def _(t, th=th):
                        t.wait_ge(s_y1bh.h, 16 * (th + 1))
                        for e in range(NEXP):
                            eg = th * NEXP + e
                            t.wait_ge(s_silu.h, 22 * eg)
                            t.wait_ge(s_uc.h, 22 * eg)
                            t.wait_ge(s_acc.h, _cl(32 * eg - 6))
                            t.wait_ge(s_crow.h, 16 * (eg + 1))
                            for tt2 in range(2):
                                t.matmul(pb(tt2), c.ones_row_bf[:, :],
                                         crow[e % 2][0:1, tt2 * 512:
                                                     tt2 * 512 + 512],
                                         start=True,
                                         stop=True).then_inc(s_cpe.h, 1)
                            for ic in range(NIC):
                                gic = eg * NIC + ic
                                p4 = 4 * (ic % 2)
                                t.wait_ge(s_w13.h, 16 * (gic + 1))
                                if ic == 0:
                                    t.wait_ge(s_crep.h, 2 * (eg + 1))
                                    t.wait_ge(s_acc.h, _cl(32 * eg - 4))
                                elif ic == 1:
                                    t.wait_ge(s_acc.h, _cl(32 * eg))
                                else:
                                    t.wait_ge(s_silu.h, 2 * (gic - 1))
                                    t.wait_ge(s_uc.h, 2 * (gic - 1))
                                mm = None
                                for gu in range(2):
                                    for fi in range(NFI):
                                        for tt2 in range(2):
                                            mm = t.matmul(
                                                pb(p4 + 2 * gu + tt2),
                                                w13buf[ic % 2][
                                                    :, (gu * NFI + fi) * 128:
                                                    (gu * NFI + fi) * 128 + 128],
                                                y1bh[:, fi * 1024 + tt2 * 512:
                                                     fi * 1024 + tt2 * 512 + 512],
                                                start=(fi == 0),
                                                stop=(fi == NFI - 1))
                                mm.then_inc(s_gupe.h, 1)
                            t.wait_ge(s_h.h, 22 * (eg + 1))
                            for fc in range(NFI):
                                gfc = eg * NFI + fc
                                t.wait_ge(s_w2.h, 16 * (gfc + 1))
                                for tt2 in range(2):
                                    gw = 2 * gfc + tt2
                                    bk = gw % 8
                                    t.wait_ge(s_acc.h, _cl(gw - 7))
                                    mm = None
                                    for ic in range(NIC):
                                        mm = t.matmul(
                                            pb(bk),
                                            w2buf[fc % 2][:, ic * 128:
                                                          ic * 128 + 128],
                                            h_sb[:, ic * 1024 + tt2 * 512:
                                                 ic * 1024 + tt2 * 512 + 512],
                                            start=(ic == 0), stop=(ic == NIC - 1))
                                    mm.then_inc(s_w2pe.h, 1)

                    @blk.scalar
                    def _(s, th=th):
                        for e in range(NEXP):
                            eg = th * NEXP + e
                            for tt2 in range(2):
                                s.wait_ge(s_cpe.h, 2 * eg + tt2 + 1)
                                if tt2 == 0:
                                    s.wait_ge(s_uc.h, 22 * eg)
                                s.copy(crep[:, tt2 * 512: tt2 * 512 + 512],
                                       pb(tt2)).then_inc(s_crep.h, 1)
                            for ic in range(NIC):
                                gic = eg * NIC + ic
                                p4 = 4 * (ic % 2)
                                s.wait_ge(s_gupe.h, gic + 1)
                                s.wait_ge(s_h.h, _cl(2 * (gic - 1)))
                                for tt2 in range(2):
                                    s.activation(sgs[(2 * gic + tt2) % 4][:, :],
                                                 pb(p4 + tt2), AF.Silu
                                                 ).then_inc(s_silu.h, 1)
                            if e == 0:
                                for fc in range(NFI):
                                    gfc = eg * NFI + fc
                                    for tt2 in range(2):
                                        gw = 2 * gfc + tt2
                                        s.wait_ge(s_w2pe.h, gw + 1)
                                        s.copy(acc[:, fc * 1024 + tt2 * 512:
                                                   fc * 1024 + tt2 * 512 + 512],
                                               pb(gw % 8)).then_inc(s_acc.h, 1)

                    @blk.vector
                    def _(v, th=th):
                        for e in range(NEXP):
                            eg = th * NEXP + e
                            for ic in range(NIC):
                                gic = eg * NIC + ic
                                p4 = 4 * (ic % 2)
                                v.wait_ge(s_gupe.h, gic + 1)
                                if ic < 2:
                                    v.wait_ge(s_w2pe.h, 32 * eg)
                                v.wait_ge(s_h.h, _cl(2 * (gic - 1)))
                                for tt2 in range(2):
                                    v.tensor_mul(ucs[(2 * gic + tt2) % 4][:, :],
                                                 crep[:, tt2 * 512:
                                                      tt2 * 512 + 512],
                                                 pb(p4 + 2 + tt2)
                                                 ).then_inc(s_uc.h, 1)
                                v.wait_ge(s_uc.h, 2 * (gic + 1))
                                v.wait_ge(s_silu.h, 2 * (gic + 1))
                                for tt2 in range(2):
                                    v.tensor_mul(h_sb[:, ic * 1024 + tt2 * 512:
                                                      ic * 1024 + tt2 * 512 + 512],
                                                 sgs[(2 * gic + tt2) % 4][:, :],
                                                 ucs[(2 * gic + tt2) % 4][:, :]
                                                 ).then_inc(s_h.h, 1)
                            if e > 0:
                                for fc in range(NFI):
                                    gfc = eg * NFI + fc
                                    for tt2 in range(2):
                                        gw = 2 * gfc + tt2
                                        v.wait_ge(s_w2pe.h, gw + 1)
                                        if fc == 0 and tt2 == 0:
                                            v.wait_ge(s_acc.h, 32 * eg)
                                        sl = acc[:, fc * 1024 + tt2 * 512:
                                                 fc * 1024 + tt2 * 512 + 512]
                                        v.tensor_add(sl, pb(gw % 8),
                                                     sl).then_inc(s_acc.h, 1)

            with ExitStack() as lctx:
                sbn = lambda n, sh, dt: lctx.enter_context(
                    nc.sbuf_tensor(n, sh, dt))
                ypre2 = sbn(f"ypre2_{th}", [128, NFI * 512], F32)
                y1res = sbn(f"y1res_{th}", [128, NFI * 512], F32)
                stat2 = sbn(f"stat2_{th}", [1, 512 * 6], F32)
                ps2 = lctx.enter_context(nc.psum_tensor(f"ps5l_{th}",
                                                        [128, 2048], F32))
                ps_sum2 = lctx.enter_context(nc.psum_tensor(f"ps_sum5_{th}",
                                                            [1, 512], F32))
                ps_sq2 = lctx.enter_context(nc.psum_tensor(f"ps_sq5_{th}",
                                                           [1, 512], F32))
                pb2 = lambda b: ps2[:, b * 512:(b + 1) * 512]
                for qt2 in range(2):
                    m = th * 2 + qt2
                    nrm0 = m * 5
                    with nc.Block() as blk:
                        @blk.sync
                        def _(sy, th=th, qt2=qt2, m=m):
                            src = bass.AP(c.t_y1, th * 1024 + qt2 * 512,
                                          [[TOK, 128], [128 * TOK, NFI],
                                           [1, 512]])
                            sy.dma_start(y1res[:, :],
                                         src).then_inc(s_res.h, 16)

                        @blk.vector
                        def _(v, th=th, qt2=qt2, m=m):
                            v.wait_ge(s_res.h, 16 * (m + 1))
                            mm = None
                            for fc in range(NFI):
                                mm = v.tensor_add(
                                    ypre2[:, fc * 512: fc * 512 + 512],
                                    acc[:, fc * 1024 + qt2 * 512:
                                        fc * 1024 + qt2 * 512 + 512],
                                    y1res[:, fc * 512: fc * 512 + 512])
                            mm.then_inc(s_add.h, 1)

                    with nc.Block() as blk:
                        @blk.scalar
                        def _(s, th=th, qt2=qt2, m=m):
                            mm = None
                            for fc in range(NFI):
                                mm = s.activation(
                                    y1res[:, fc * 512: fc * 512 + 512],
                                    ypre2[:, fc * 512: fc * 512 + 512],
                                    AF.Square)
                            mm.then_inc(s_sq.h, 1)
                            s.wait_ge(s_nrm.h, nrm0 + 5)
                            s.activation(stat2[0:1, 2560:3072],
                                         stat2[0:1, 2048:2560],
                                         AF.Sqrt).then_inc(s_rstd.h, 1)

                        @blk.tensor
                        def _(t, th=th, qt2=qt2, m=m):
                            for fc in range(NFI):
                                t.matmul(ps_sum2[0:1, :], c.ones_col_f[:, :],
                                         ypre2[:, fc * 512: fc * 512 + 512],
                                         start=(fc == 0), stop=(fc == NFI - 1))
                            t.wait_ge(s_sq.h, m + 1)
                            mm = None
                            for fc in range(NFI):
                                mm = t.matmul(ps_sq2[0:1, :],
                                              c.ones_col_f[:, :],
                                              y1res[:, fc * 512: fc * 512 + 512],
                                              start=(fc == 0),
                                              stop=(fc == NFI - 1))
                            mm.then_inc(s_stpe.h, 1)
                            t.wait_ge(s_nrm.h, nrm0 + 1)
                            t.matmul(pb2(0), c.ones_row_f[:, :],
                                     stat2[0:1, 0:512], start=True, stop=True)
                            t.wait_ge(s_rstd.h, m + 1)
                            t.matmul(pb2(1), c.ones_row_f[:, :],
                                     stat2[0:1, 2560:3072], start=True,
                                     stop=True).then_inc(s_bc.h, 1)

                        @blk.vector
                        def _(v, th=th, qt2=qt2, m=m):
                            v.wait_ge(s_stpe.h, m + 1)
                            v.tensor_scalar_mul(stat2[0:1, 0:512],
                                                ps_sum2[0:1, :], INV_D)
                            v.tensor_scalar_mul(stat2[0:1, 512:1024],
                                                ps_sq2[0:1, :],
                                                INV_D).then_inc(s_nrm.h, 1)
                            v.wait_ge(s_nrm.h, nrm0 + 1)
                            v.tensor_mul(stat2[0:1, 1024:1536],
                                         stat2[0:1, 0:512],
                                         stat2[0:1, 0:512]).then_inc(s_nrm.h, 1)
                            v.wait_ge(s_nrm.h, nrm0 + 2)
                            v.tensor_sub(stat2[0:1, 1536:2048],
                                         stat2[0:1, 512:1024],
                                         stat2[0:1, 1024:1536]
                                         ).then_inc(s_nrm.h, 1)
                            v.wait_ge(s_nrm.h, nrm0 + 3)
                            v.tensor_scalar_add(stat2[0:1, 2048:2560],
                                                stat2[0:1, 1536:2048],
                                                EPS).then_inc(s_nrm.h, 1)
                            v.wait_ge(s_nrm.h, nrm0 + 4)
                            v.reciprocal(stat2[0:1, 2048:2560],
                                         stat2[0:1, 2048:2560]
                                         ).then_inc(s_nrm.h, 1)
                            v.wait_ge(s_bc.h, m + 1)
                            mm = None
                            for fc in range(NFI):
                                mm = v.tensor_sub(
                                    ypre2[:, fc * 512: fc * 512 + 512],
                                    ypre2[:, fc * 512: fc * 512 + 512], pb2(0))
                            mm.then_inc(s_sub.h, 1)
                            v.wait_ge(s_sub.h, m + 1)
                            mm = None
                            for fc in range(NFI):
                                mm = v.tensor_mul(
                                    ypre2[:, fc * 512: fc * 512 + 512],
                                    ypre2[:, fc * 512: fc * 512 + 512], pb2(1))
                            mm.then_inc(s_y2.h, 1)

                        @blk.sync
                        def _(sy, th=th, qt2=qt2, m=m):
                            sy.wait_ge(s_y2.h, m + 1)
                            dst = bass.AP(c.t_out, th * 1024 + qt2 * 512,
                                          [[TOK, 128], [128 * TOK, NFI],
                                           [1, 512]])
                            sy.dma_start(dst, ypre2[:, :]).then_inc(s_out.h, 16)
                            sy.wait_ge(s_out.h, 16 * (m + 1))


_NC_CACHE = {}


def kernel(**inputs):
    per_core = _host_prep(inputs)
    key = (LAST_PHASE, tuple(sorted(DBG)))
    if key not in _NC_CACHE:
        _NC_CACHE[key] = build_nc()
    nc = _NC_CACHE[key]
    res = run_bass_kernel_spmd(nc, per_core, core_ids=list(range(8)))
    return _assemble([r["out"] for r in res.results])


if __name__ == "__main__":
    import reference as _ref
    inp = _ref.setup_inputs()
    y = kernel(**{k: np.asarray(v) for k, v in inp.items()})
    ref = np.asarray(_ref.reference(**inp))
    num = float(((y - ref) ** 2).mean())
    den = float((ref ** 2).mean())
    print("resvar", num / den)


# version 04
# speedup vs baseline: 1.1088x; 1.0710x over previous
import numpy as np
import ml_dtypes
from contextlib import ExitStack
import concourse.bass as bass
import concourse.mybir as mybir
from concourse.bass_utils import run_bass_kernel_spmd

F32 = mybir.dt.float32
BF16 = mybir.dt.bfloat16
AF = mybir.ActivationFunctionType
ALU = mybir.AluOpType
AX = mybir.AxisListType

B, S, D = 2, 8192, 2048
H, HD = 16, 128
NFI = D // 128
TOK = 2048
NSLOT = 16
NEXP = 20
NIC = 11
IEXP = 1408
SCALE = float(HD) ** -0.5
EPS = 1e-5
NEG_BIG = -1e30
INV_D = 1.0 / D

LAST_PHASE = "p5"
DBG = ()
P2_LIMIT = None


def _cl(v):
    return max(0, v)


def _host_prep(inp):
    f32 = np.float32
    x = np.asarray(inp["L0_attn_hidden_states"], f32)

    def wimg_sq(w):
        wt = np.ascontiguousarray(np.asarray(w, f32).T)
        return np.ascontiguousarray(
            wt.reshape(NFI, 128, NFI, 128).transpose(1, 0, 2, 3).reshape(128, NFI * D))

    wq_img = wimg_sq(inp["L0_attn_Wq"])
    wk_img = wimg_sq(inp["L0_attn_Wk"])
    wv_img = wimg_sq(inp["L0_attn_Wv"])
    wo_img = wimg_sq(inp["L0_attn_Wo"])

    gs = np.concatenate([np.asarray(inp["L0_ffn_gate_w"], f32),
                         np.asarray(inp["L0_ffn_sgate_w"], f32)], axis=0)
    gs_img = np.ascontiguousarray(
        gs.T.reshape(NFI, 128, 17).transpose(1, 0, 2).reshape(128, NFI * 17))

    w13 = np.asarray(inp["L0_ffn_w13"], f32)
    w2 = np.asarray(inp["L0_ffn_w2"], f32)
    sgu = np.asarray(inp["L0_ffn_shared_gate_up_w"], f32)
    sdw = np.asarray(inp["L0_ffn_shared_down_w"], f32)
    w13_list = [w13[e] for e in range(16)]
    w2_list = [w2[e] for e in range(16)]
    for p in range(4):
        g_p = sgu[p * IEXP:(p + 1) * IEXP]
        u_p = sgu[5632 + p * IEXP: 5632 + (p + 1) * IEXP]
        w13_list.append(np.concatenate([g_p, u_p], axis=0))
        w2_list.append(sdw[:, p * IEXP:(p + 1) * IEXP])

    w13_img = np.empty((NEXP, NIC, 128, 2 * NFI * 128), f32)
    for e in range(NEXP):
        a = w13_list[e].T.reshape(NFI, 128, 2, NIC, 128)
        w13_img[e] = a.transpose(3, 1, 2, 0, 4).reshape(NIC, 128, 2 * NFI * 128)
    w2_img = np.empty((NEXP, NFI, 128, NIC * 128), f32)
    for e in range(NEXP):
        a = w2_list[e].T.reshape(NIC, 128, NFI, 128)
        w2_img[e] = a.transpose(2, 1, 0, 3).reshape(NFI, 128, NIC * 128)

    ident = np.eye(128, dtype=f32)

    per_core = []
    for cid in range(8):
        b, j = cid // 4, cid % 4
        xT = np.ascontiguousarray(x[b].T)
        xseq = xT.reshape(NFI, 128, S)
        own_cols = ((j + 4 * (np.arange(TOK) // 128)) * 128 + np.arange(TOK) % 128)
        xown = np.ascontiguousarray(xT[:, own_cols]).reshape(NFI, 128, TOK)
        p_idx = np.arange(128)[:, None]
        cq = np.arange(4 * 128)[None, :]
        cc, qq = cq // 128, cq % 128
        maskT = np.ascontiguousarray(
            np.where(128 * cc + p_idx <= 128 * j + qq, 0.0, NEG_BIG).astype(f32))
        per_core.append({"xseq": xseq, "xown": xown, "maskT": maskT,
                         "wq": wq_img, "wk": wk_img, "wv": wv_img, "wo": wo_img,
                         "gs": gs_img, "w13": w13_img, "w2": w2_img, "ident": ident})
    return per_core


def _assemble(outs):
    y = np.empty((B, S, D), np.float32)
    for cid in range(8):
        b, j = cid // 4, cid % 4
        o = np.asarray(outs[cid])
        yt = o.reshape(NFI, 128, NSLOT, 128).transpose(2, 3, 0, 1).reshape(TOK, D)
        y[b].reshape(64, 128, D)[j::4] = yt.reshape(NSLOT, 128, D)
    return y


class _C:
    pass


class _Sem:
    __slots__ = ("h",)

    def __init__(self, h):
        self.h = h


def build_nc():
    nc = bass.Bass("TRN2", target_bir_lowering=False)
    c = _C()
    c.nc = nc
    c.t_xseq = nc.dram_tensor("xseq", [NFI, 128, S], F32, kind="ExternalInput")
    c.t_xown = nc.dram_tensor("xown", [NFI, 128, TOK], F32, kind="ExternalInput")
    c.t_mask = nc.dram_tensor("maskT", [128, 512], F32, kind="ExternalInput")
    c.t_wq = nc.dram_tensor("wq", [128, NFI * D], F32, kind="ExternalInput")
    c.t_wk = nc.dram_tensor("wk", [128, NFI * D], F32, kind="ExternalInput")
    c.t_wv = nc.dram_tensor("wv", [128, NFI * D], F32, kind="ExternalInput")
    c.t_wo = nc.dram_tensor("wo", [128, NFI * D], F32, kind="ExternalInput")
    c.t_gs = nc.dram_tensor("gs", [128, NFI * 17], F32, kind="ExternalInput")
    c.t_w13 = nc.dram_tensor("w13", [NEXP, NIC, 128, 2 * NFI * 128], F32,
                             kind="ExternalInput")
    c.t_w2 = nc.dram_tensor("w2", [NEXP, NFI, 128, NIC * 128], F32,
                            kind="ExternalInput")
    c.t_id = nc.dram_tensor("ident", [128, 128], F32, kind="ExternalInput")
    c.t_out = nc.dram_tensor("out", [NFI * 128, TOK], F32, kind="ExternalOutput")

    def scr(name, shape, dt, tag):
        kind = "ExternalOutput" if tag in DBG else "Internal"
        return nc.dram_tensor(name, shape, dt, kind=kind)

    c.t_kscr = scr("k_scr", [H, 128, S], BF16, "kv")
    c.t_vscr = scr("v_scr", [S, D], BF16, "kv")
    c.t_y1 = scr("y1_scr", [NFI, 128, TOK], F32, "y1")
    c.t_y1b = scr("y1b_scr", [NFI, 128, TOK], BF16, "y1b")
    c.t_coef = scr("coef_scr", [17, TOK], BF16, "coef")
    c.t_dbg = {}
    for name, shape, dt in [("q", [128, H * TOK], BF16),
                            ("attn", [128, H * TOK], BF16),
                            ("mhat", [128, H], F32)]:
        if name in DBG:
            c.t_dbg[name] = nc.dram_tensor("dbg_" + name, shape, dt,
                                           kind="ExternalOutput")

    with ExitStack() as top:
        c.top = top
        sb = lambda n, sh, dt: top.enter_context(nc.sbuf_tensor(n, sh, dt))
        c.ones_col_bf = sb("ones_col_bf", [128, 1], BF16)
        c.ones_col_f = sb("ones_col_f", [128, 1], F32)
        c.ones_row_f = sb("ones_row_f", [1, 128], F32)
        c.ones_row_bf = sb("ones_row_bf", [1, 128], BF16)
        c.id_bf = sb("id_bf", [128, 128], BF16)
        c.mask_sb = sb("mask_sb", [128, 512], F32)
        c.mhat_neg = sb("mhat_neg", [128, H], F32)
        c.nrm_k = sb("nrm_k", [1, 256], F32)
        c.nrm_q = sb("nrm_q", [1, 64], F32)
        c.sem_i = 0

        def sem(n):
            c.sem_i += 1
            return _Sem(top.enter_context(nc.semaphore(f"{n}_{c.sem_i}")))
        c.sem = sem

        _p0_consts(c)
        if LAST_PHASE == "p0":
            return nc
        _p1_proj(c, "k")
        if LAST_PHASE == "p1k":
            return nc
        _p1_proj(c, "v")
        if LAST_PHASE == "p1v":
            return nc
        with ExitStack() as attn_scope:
            c.attn_fm = attn_scope.enter_context(
                nc.sbuf_tensor("attn_fm", [128, H * TOK], BF16))
            with ExitStack() as q_scope:
                c.q_fm = q_scope.enter_context(
                    nc.sbuf_tensor("q_fm", [128, H * TOK], BF16))
                _p1_q(c, q_scope)
                if LAST_PHASE == "p1q":
                    return nc
                _p1_mhat(c)
                _dump(c, [("q", c.q_fm), ("mhat", c.mhat_neg)])
                if LAST_PHASE == "p1":
                    return nc
                _p2_attn(c)
            _dump(c, [("attn", c.attn_fm)])
            if LAST_PHASE == "p2":
                return nc
            _p3(c)
        if LAST_PHASE == "p3":
            return nc
        _p5(c)
    return nc


def _dump(c, pairs):
    pairs = [(n, sbuf) for (n, sbuf) in pairs if n in c.t_dbg]
    if not pairs:
        return
    s = c.sem("s_dump")
    with c.nc.Block() as blk:
        @blk.sync
        def _(sy):
            n = 0
            for name, sbuf in pairs:
                sy.dma_start(c.t_dbg[name].ap(), sbuf[:, :]).then_inc(s.h, 16)
                n += 16
            sy.wait_ge(s.h, n)


def _p0_consts(c):
    nc = c.nc
    s = c.sem("s_init")
    with nc.Block() as blk:
        @blk.gpsimd
        def _(g):
            g.memset(c.ones_col_bf[:, :], 1.0)
            g.memset(c.ones_col_f[:, :], 1.0)
            g.memset(c.ones_row_f[:, :], 1.0)
            g.memset(c.ones_row_bf[:, :], 1.0)
            g.dma_start(c.id_bf[:, :], c.t_id.ap()).then_inc(s.h, 16)
            g.dma_start(c.mask_sb[:, :], c.t_mask.ap()).then_inc(s.h, 16)
            g.wait_ge(s.h, 32)


def _p1_proj(c, which):
    nc = c.nc
    NTT = S // 512
    is_k = which == "k"
    t_w = c.t_wk if is_k else c.t_wv
    t_scr = c.t_kscr if is_k else c.t_vscr
    NG = 8 if is_k else 4
    with ExitStack() as ctx:
        sbl = lambda n, sh, dt: ctx.enter_context(nc.sbuf_tensor(n, sh, dt))
        w_half = sbl(f"w_{which}", [128, 8 * NFI * 128], BF16)
        xbuf = [sbl(f"x_{which}{i}", [128, NFI * 512], BF16) for i in range(2)]
        ew = 512 if is_k else 1024
        evac = [sbl(f"ev_{which}{i}", [128, ew], BF16) for i in range(4)]
        ksq = [sbl(f"ksq{i}", [128, 512], BF16) for i in range(2)] if is_k else None
        ps = ctx.enter_context(nc.psum_tensor(f"ps_{which}", [128, 4096], F32))
        pb = lambda b: ps[:, b * 512:(b + 1) * 512]
        s_w = c.sem("sw")
        s_x = c.sem("sx")
        s_pe = c.sem("spe")
        s_ev = c.sem("sev")
        s_npe = c.sem("snpe")
        s_ndve = c.sem("sndve")
        s_st = c.sem("sst")

        def norm_mm(t, i):
            t.wait_ge(s_ev.h, i + 1)
            t.wait_ge(s_ndve.h, _cl(i - 1))
            bank = 6 + (i % 2)
            t.matmul(ps[0:1, bank * 512:(bank + 1) * 512], c.ones_col_bf[:, :],
                     ksq[i % 2][:, :], start=True, stop=True).then_inc(s_npe.h, 1)

        for hf in range(2):
            with nc.Block() as blk:
                @blk.gpsimd
                def _(g, hf=hf):
                    src = bass.AP(t_w, hf * 8 * 128,
                                  [[NFI * D, 128], [D, NFI], [1, 8 * 128]])
                    g.dma_start(w_half[:, :], src).then_inc(s_w.h, 16)
                    for tt in range(NTT):
                        gi = hf * NTT + tt
                        if gi >= 2:
                            g.wait_ge(s_pe.h, (gi - 1) * NG)
                        src = bass.AP(c.t_xseq, tt * 512,
                                      [[S, 128], [128 * S, NFI], [1, 512]])
                        g.dma_start(xbuf[gi % 2][:, :], src).then_inc(s_x.h, 16)

                @blk.tensor
                def _(t, hf=hf):
                    t.wait_ge(s_w.h, 16 * (hf + 1))
                    for tt in range(NTT):
                        gi = hf * NTT + tt
                        t.wait_ge(s_x.h, 16 * (gi + 1))
                        for cg in range(NG):
                            i = gi * NG + cg
                            t.wait_ge(s_ev.h, _cl(i - 1))
                            mm = None
                            if is_k:
                                for fi in range(NFI):
                                    mm = t.matmul(
                                        pb(i % 2),
                                        w_half[:, (fi * 8 + cg) * 128:
                                               (fi * 8 + cg) * 128 + 128],
                                        xbuf[gi % 2][:, fi * 512: fi * 512 + 512],
                                        start=(fi == 0), stop=(fi == NFI - 1))
                            else:
                                for fi in range(NFI):
                                    for b2 in range(2):
                                        mm = t.matmul(
                                            pb(2 * (i % 2) + b2),
                                            xbuf[gi % 2][:, fi * 512 + cg * 128:
                                                         fi * 512 + cg * 128 + 128],
                                            w_half[:, (fi * 8 + 4 * b2) * 128:
                                                   (fi * 8 + 4 * b2) * 128 + 512],
                                            start=(fi == 0), stop=(fi == NFI - 1))
                            mm.then_inc(s_pe.h, 1)
                            if is_k and i >= 1:
                                norm_mm(t, i - 1)
                    if is_k and hf == 1:
                        norm_mm(t, 2 * NTT * 8 - 1)

                @blk.scalar
                def _(s, hf=hf):
                    for tt in range(NTT):
                        gi = hf * NTT + tt
                        for cg in range(NG):
                            i = gi * NG + cg
                            s.wait_ge(s_pe.h, i + 1)
                            s.wait_ge(s_st.h, 16 * _cl(i - 3))
                            if is_k:
                                s.wait_ge(s_npe.h, _cl(i - 1))
                                s.copy(evac[i % 4][:, :], pb(i % 2))
                                s.activation(ksq[i % 2][:, :], pb(i % 2),
                                             AF.Square).then_inc(s_ev.h, 1)
                            else:
                                s.copy(evac[i % 4][:, 0:512], pb(2 * (i % 2)))
                                s.copy(evac[i % 4][:, 512:1024],
                                       pb(2 * (i % 2) + 1)).then_inc(s_ev.h, 1)

                if is_k:
                    @blk.vector
                    def _(v, hf=hf):
                        lo = 0 if hf == 0 else 127
                        hi = 127 if hf == 0 else 256
                        for i in range(lo, hi):
                            v.wait_ge(s_npe.h, i + 1)
                            bank = 6 + (i % 2)
                            gi2, cg2 = i // 8, i % 8
                            hf2, tt2 = gi2 // NTT, gi2 % NTT
                            slot = (hf2 * 8 + cg2) * 16 + tt2
                            v.reduce_max(c.nrm_k[:, slot:slot + 1],
                                         ps[0:1, bank * 512:(bank + 1) * 512],
                                         axis=AX.X).then_inc(s_ndve.h, 1)

                @blk.sync
                def _(sy, hf=hf):
                    for tt in range(NTT):
                        gi = hf * NTT + tt
                        for cg in range(NG):
                            i = gi * NG + cg
                            sy.wait_ge(s_ev.h, i + 1)
                            if is_k:
                                fc = hf * 8 + cg
                                dst = bass.AP(t_scr, fc * 128 * S + tt * 512,
                                              [[S, 128], [1, 512]])
                            else:
                                row0 = tt * 512 + cg * 128
                                dst = bass.AP(t_scr, row0 * D + hf * 1024,
                                              [[D, 128], [1, 1024]])
                            sy.dma_start(dst, evac[i % 4][:, :]).then_inc(s_st.h, 16)
                    if hf == 1:
                        sy.wait_ge(s_st.h, 16 * 2 * NTT * NG)


def _p1_q(c, scope):
    nc = c.nc
    NTT = 4
    with ExitStack() as ctx:
        sbl = lambda n, sh, dt: ctx.enter_context(nc.sbuf_tensor(n, sh, dt))
        w_half = sbl("w_q", [128, 8 * NFI * 128], BF16)
        xbuf = [sbl(f"x_q{i}", [128, NFI * 512], BF16) for i in range(2)]
        qsq = [sbl(f"qsq{i}", [128, 512], BF16) for i in range(2)]
        ps = ctx.enter_context(nc.psum_tensor("ps_q", [128, 4096], F32))
        pb = lambda b: ps[:, b * 512:(b + 1) * 512]
        s_w = c.sem("sqw")
        s_x = c.sem("sqx")
        s_pe = c.sem("sqpe")
        s_ev = c.sem("sqev")
        s_npe = c.sem("sqnpe")
        s_ndve = c.sem("sqndve")

        def norm_mm(t, i):
            t.wait_ge(s_ev.h, i + 1)
            t.wait_ge(s_ndve.h, _cl(i - 1))
            bank = 6 + (i % 2)
            t.matmul(ps[0:1, bank * 512:(bank + 1) * 512], c.ones_col_bf[:, :],
                     qsq[i % 2][:, :], start=True, stop=True).then_inc(s_npe.h, 1)

        for hf in range(2):
            with nc.Block() as blk:
                @blk.gpsimd
                def _(g, hf=hf):
                    src = bass.AP(c.t_wq, hf * 8 * 128,
                                  [[NFI * D, 128], [D, NFI], [1, 8 * 128]])
                    g.dma_start(w_half[:, :], src).then_inc(s_w.h, 16)
                    for tt in range(NTT):
                        gi = hf * NTT + tt
                        if gi >= 2:
                            g.wait_ge(s_pe.h, (gi - 1) * 8)
                        src = bass.AP(c.t_xown, tt * 512,
                                      [[TOK, 128], [128 * TOK, NFI], [1, 512]])
                        g.dma_start(xbuf[gi % 2][:, :], src).then_inc(s_x.h, 16)

                @blk.tensor
                def _(t, hf=hf):
                    t.wait_ge(s_w.h, 16 * (hf + 1))
                    for tt in range(NTT):
                        gi = hf * NTT + tt
                        t.wait_ge(s_x.h, 16 * (gi + 1))
                        for cg in range(8):
                            i = gi * 8 + cg
                            t.wait_ge(s_ev.h, _cl(i - 1))
                            mm = None
                            for fi in range(NFI):
                                mm = t.matmul(
                                    pb(i % 2),
                                    w_half[:, (fi * 8 + cg) * 128:
                                           (fi * 8 + cg) * 128 + 128],
                                    xbuf[gi % 2][:, fi * 512: fi * 512 + 512],
                                    start=(fi == 0), stop=(fi == NFI - 1))
                            mm.then_inc(s_pe.h, 1)
                            if i >= 1:
                                norm_mm(t, i - 1)
                    if hf == 1:
                        norm_mm(t, 2 * NTT * 8 - 1)

                @blk.scalar
                def _(s, hf=hf):
                    for tt in range(NTT):
                        gi = hf * NTT + tt
                        for cg in range(8):
                            i = gi * 8 + cg
                            fc = hf * 8 + cg
                            s.wait_ge(s_pe.h, i + 1)
                            s.wait_ge(s_npe.h, _cl(i - 1))
                            s.activation(
                                c.q_fm[:, fc * TOK + tt * 512:
                                       fc * TOK + tt * 512 + 512],
                                pb(i % 2), AF.Copy, scale=SCALE)
                            s.activation(qsq[i % 2][:, :], pb(i % 2),
                                         AF.Square, scale=SCALE).then_inc(s_ev.h, 1)

                @blk.vector
                def _(v, hf=hf):
                    lo = 0 if hf == 0 else 31
                    hi = 31 if hf == 0 else 64
                    for i in range(lo, hi):
                        v.wait_ge(s_npe.h, i + 1)
                        bank = 6 + (i % 2)
                        gi2, cg2 = i // 8, i % 8
                        hf2, tt2 = gi2 // NTT, gi2 % NTT
                        slot = (hf2 * 8 + cg2) * 4 + tt2
                        v.reduce_max(c.nrm_q[:, slot:slot + 1],
                                     ps[0:1, bank * 512:(bank + 1) * 512],
                                     axis=AX.X).then_inc(s_ndve.h, 1)


def _p1_mhat(c):
    nc = c.nc
    with ExitStack() as ctx:
        sbl = lambda n, sh, dt: ctx.enter_context(nc.sbuf_tensor(n, sh, dt))
        km = sbl("km2", [1, H], F32)
        qm = sbl("qm2", [1, H], F32)
        m2 = sbl("m2", [1, H], F32)
        mh = sbl("mh", [1, H], F32)
        mhn = sbl("mhn", [1, H], F32)
        ps = ctx.enter_context(nc.psum_tensor("ps_mh", [128, 512], F32))
        s_a = c.sem("sma")
        s_b = c.sem("smb")
        s_d = c.sem("smd")
        with nc.Block() as blk:
            @blk.vector
            def _(v):
                mm = None
                for fc in range(H):
                    v.reduce_max(km[:, fc:fc + 1],
                                 c.nrm_k[0:1, fc * 16: fc * 16 + 16], axis=AX.X)
                    mm = v.reduce_max(qm[:, fc:fc + 1],
                                      c.nrm_q[0:1, fc * 4: fc * 4 + 4],
                                      axis=AX.X)
                mm.then_inc(s_a.h, 1)
                v.wait_ge(s_a.h, 1)
                v.tensor_mul(m2[:, :], km[:, :], qm[:, :]).then_inc(s_a.h, 1)

            @blk.scalar
            def _(s):
                s.wait_ge(s_a.h, 2)
                s.activation(mh[:, :], m2[:, :], AF.Sqrt).then_inc(s_b.h, 1)
                s.wait_ge(s_b.h, 1)
                s.activation(mhn[:, :], mh[:, :], AF.Copy,
                             scale=-1.0).then_inc(s_b.h, 2)

            @blk.tensor
            def _(t):
                t.wait_ge(s_b.h, 3)
                t.matmul(ps[:, 0:H], c.ones_row_f[:, :], mhn[:, :],
                         start=True, stop=True).then_inc(s_d.h, 1)
        with nc.Block() as blk2:
            @blk2.scalar
            def _(s):
                s.copy(c.mhat_neg[:, :], ps[:, 0:H])


def _p2_schedule():
    units, ktloads, ghps = [], [], []
    u = kl = 0
    for g in range(4):
        for hp in range(8):
            ghp = {"g": g, "hp": hp, "idx": len(ghps), "u0": u, "units": [],
                   "kls": []}
            for kt in range(4 * g + 4):
                rec = {"g": g, "hp": hp, "kt": kt, "idx": kl, "u0": u}
                for qb in range(4 * g, 4 * g + 4):
                    if qb < kt:
                        continue
                    for h2 in range(2):
                        un = {"idx": u, "g": g, "hp": hp, "kt": kt, "qb": qb,
                              "h2": h2, "kl": kl, "diag": kt == qb,
                              "first": kt == 0, "last": kt == qb,
                              "r": (qb - 4 * g) * 2 + h2}
                        units.append(un)
                        ghp["units"].append(un)
                        u += 1
                rec["u1"] = u
                ktloads.append(rec)
                ghp["kls"].append(rec)
                kl += 1
            ghp["u1"] = u
            ghps.append(ghp)
    nd = 0
    for un in units:
        if un["diag"]:
            nd += 1
        un["ndiag"] = nd
    return units, ktloads, ghps


def _p2_attn(c):
    nc = c.nc
    units, ktloads, ghps = _p2_schedule()
    if P2_LIMIT is not None:
        ghps = ghps[:P2_LIMIT]
        units = [u for g in ghps for u in g["units"]]
        kls = set()
        for g in ghps:
            for r in g["kls"]:
                kls.add(r["idx"])
        ktloads = [r for r in ktloads if r["idx"] in kls]
    with ExitStack() as ctx:
        sbl = lambda n, sh, dt: ctx.enter_context(nc.sbuf_tensor(n, sh, dt))
        kbuf = [sbl(f"kbuf{i}", [128, 1024], BF16) for i in range(2)]
        vbuf = [sbl(f"vbuf{i}", [128, 1024], BF16) for i in range(2)]
        ptb = [sbl(f"ptb{i}", [128, 512], BF16) for i in range(2)]
        linv = sbl("linv", [1, 1024], F32)
        linv_rep = sbl("linv_rep", [128, 1024], F32)
        ps = ctx.enter_context(nc.psum_tensor("ps_at", [128, 4096], F32))
        s_kv = c.sem("skv")
        s_spe = c.sem("sspe")
        s_mask = c.sem("smask")
        s_exp = c.sem("sexp")
        s_av = c.sem("sav")
        s_rec = c.sem("srec")
        s_bc = c.sem("sbc")
        s_lr = c.sem("slr")
        s_fin = c.sem("sfin")

        st_ps = lambda u: ps[:, (u % 2) * 512:(u % 2) * 512 + 512]

        def av_ps(r):
            off = (4 + r // 4) * 512 + (r % 4) * 128
            return ps[:, off: off + 128]

        def l_ps(r):
            off = (6 + r // 4) * 512 + (r % 4) * 128
            return ps[0:1, off: off + 128]

        def bc_ps(r):
            off = (2 + r % 2) * 512 + (r // 2) * 128
            return ps[:, off: off + 128]

        def lav(t, un):
            u, r, h2 = un["idx"], un["r"], un["h2"]
            t.wait_ge(s_exp.h, u + 1)
            first, last = un["first"], un["last"]
            st = first and (r % 4 == 0)
            for cc in range(4):
                t.matmul(l_ps(r), c.ones_col_bf[:, :],
                         ptb[u % 2][:, cc * 128: cc * 128 + 128],
                         start=(st and cc == 0), stop=(last and cc == 3))
            mm = None
            for cc in range(4):
                mm = t.matmul(av_ps(r),
                              vbuf[un["kl"] % 2][:, cc * 256 + h2 * 128:
                                                 cc * 256 + h2 * 128 + 128],
                              ptb[u % 2][:, cc * 128: cc * 128 + 128],
                              start=(st and cc == 0), stop=(last and cc == 3))
            mm.then_inc(s_av.h, 1)

        with nc.Block() as blk:
            @blk.sync
            def _(sy):
                for rec in ktloads:
                    i = rec["idx"]
                    if i >= 2:
                        sy.wait_ge(s_av.h, ktloads[i - 2]["u1"])
                    hp, kt = rec["hp"], rec["kt"]
                    srck = bass.AP(c.t_kscr, (2 * hp) * 128 * S + kt * 512,
                                   [[S, 128], [128 * S, 2], [1, 512]])
                    sy.dma_start(kbuf[i % 2][:, :], srck).then_inc(s_kv.h, 16)
                    srcv = bass.AP(c.t_vscr, (kt * 512) * D + hp * 256,
                                   [[D, 128], [128 * D, 4], [1, 256]])
                    sy.dma_start(vbuf[i % 2][:, :], srcv).then_inc(s_kv.h, 16)

            @blk.tensor
            def _(t):
                for ghp in ghps:
                    for un in ghp["units"]:
                        u = un["idx"]
                        if u == ghp["u0"]:
                            t.wait_ge(s_rec.h, 8 * ghp["idx"])
                            t.wait_ge(s_fin.h, 8 * ghp["idx"])
                        t.wait_ge(s_kv.h, 32 * (un["kl"] + 1))
                        t.wait_ge(s_exp.h, _cl(u - 1))
                        hg = 2 * un["hp"] + un["h2"]
                        qb = un["qb"]
                        mm = None
                        for cc in range(4):
                            mm = t.matmul(
                                st_ps(u)[:, cc * 128:cc * 128 + 128],
                                kbuf[un["kl"] % 2][:, un["h2"] * 512 + cc * 128:
                                                   un["h2"] * 512 + cc * 128 + 128],
                                c.q_fm[:, hg * TOK + qb * 128:
                                       hg * TOK + qb * 128 + 128],
                                start=True, stop=True)
                        mm.then_inc(s_spe.h, 1)
                        if u > ghp["u0"]:
                            lav(t, units[u - 1])
                    lav(t, units[ghp["u1"] - 1])
                    for r2 in range(8):
                        t.wait_ge(s_rec.h, 8 * ghp["idx"] + r2 + 1)
                        t.wait_ge(s_lr.h, 8 * ghp["idx"] + _cl(r2 - 1))
                        t.matmul(bc_ps(r2), c.ones_row_f[:, :],
                                 linv[0:1, r2 * 128: r2 * 128 + 128],
                                 start=True, stop=True).then_inc(s_bc.h, 1)

            @blk.vector
            def _(v):
                for ghp in ghps:
                    for un in ghp["units"]:
                        if not un["diag"]:
                            continue
                        u = un["idx"]
                        v.wait_ge(s_spe.h, u + 1)
                        v.tensor_add(st_ps(u), st_ps(u),
                                     c.mask_sb[:, :]).then_inc(s_mask.h, 1)
                    v.wait_ge(s_av.h, ghp["u1"])
                    for r in range(8):
                        v.reciprocal(linv[0:1, r * 128: r * 128 + 128],
                                     l_ps(r)).then_inc(s_rec.h, 1)
                    for r in range(8):
                        v.wait_ge(s_lr.h, 8 * ghp["idx"] + r + 1)
                        qb = 4 * ghp["g"] + r // 2
                        hg = 2 * ghp["hp"] + r % 2
                        v.tensor_mul(
                            c.attn_fm[:, hg * TOK + qb * 128:
                                      hg * TOK + qb * 128 + 128],
                            av_ps(r),
                            linv_rep[:, r * 128: r * 128 + 128]).then_inc(s_fin.h, 1)

            @blk.scalar
            def _(s):
                for ghp in ghps:
                    for un in ghp["units"]:
                        u = un["idx"]
                        hg = 2 * un["hp"] + un["h2"]
                        if un["diag"]:
                            s.wait_ge(s_mask.h, un["ndiag"])
                        else:
                            s.wait_ge(s_spe.h, u + 1)
                        s.wait_ge(s_av.h, _cl(u - 1))
                        s.activation(ptb[u % 2][:, :], st_ps(u), AF.Exp,
                                     bias=c.mhat_neg[:, hg:hg + 1]
                                     ).then_inc(s_exp.h, 1)
                    for r in range(8):
                        s.wait_ge(s_bc.h, 8 * ghp["idx"] + r + 1)
                        s.copy(linv_rep[:, r * 128: r * 128 + 128],
                               bc_ps(r)).then_inc(s_lr.h, 1)


def _p3(c):
    nc = c.nc
    with ExitStack() as ctx:
        sbl = lambda n, sh, dt: ctx.enter_context(nc.sbuf_tensor(n, sh, dt))
        wobuf = [sbl(f"wob{i}", [128, NFI * 128], BF16) for i in range(4)]
        gs_sb = sbl("gs_sb", [128, NFI * 17], F32)
        ypre = sbl("ypre3", [128, NFI * 512], F32)
        y1f = sbl("y1f3", [128, NFI * 512], F32)
        y1b = sbl("y1b3", [128, NFI * 512], BF16)
        stat = sbl("stat3", [1, 512 * 6], F32)
        probs = sbl("probs3", [128, 16], F32)
        prn = sbl("prn3", [128, 16], F32)
        wk4 = [sbl(f"wk43{i}", [128, 16], F32) for i in range(2)]
        mx = sbl("mx3", [128, 1], F32)
        nmx = sbl("nmx3", [128, 1], F32)
        eq = sbl("eq3", [128, 16], F32)
        coef_f = sbl("coef_f3", [128, 16], F32)
        coef_bf = sbl("coef_bf3", [128, 17], BF16)
        coefT = [sbl(f"coefT3{i}", [32, 128], BF16) for i in range(2)]
        sgc = sbl("sgc3", [128, 1], F32)
        lsum = sbl("lsum3", [128, 1], F32)
        lin = sbl("lin3", [128, 1], F32)
        ps = ctx.enter_context(nc.psum_tensor("ps_p3", [128, 2048], F32))
        ps_sum = ctx.enter_context(nc.psum_tensor("ps_sum3", [1, 512], F32))
        ps_sq = ctx.enter_context(nc.psum_tensor("ps_sq3", [1, 512], F32))
        ps_rt = ctx.enter_context(nc.psum_tensor("ps_rt3", [128, 68], F32))
        tr_ps = ctx.enter_context(nc.psum_tensor("ps_tr3", [128, 128], BF16))
        pb = lambda b: ps[:, b * 512:(b + 1) * 512]

        s_w = c.sem("s3w")
        s_gs = c.sem("s3gs")
        s_x = c.sem("s3x")
        s_ope = c.sem("s3ope")
        s_yp = c.sem("s3yp")
        s_sq = c.sem("s3sq")
        s_stpe = c.sem("s3stpe")
        s_nrm = c.sem("s3nrm")
        s_rstd = c.sem("s3rstd")
        s_bc = c.sem("s3bc")
        s_sub = c.sem("s3sub")
        s_y1 = c.sem("s3y1")
        s_y1b = c.sem("s3y1b")
        s_st = c.sem("s3st")
        s_rpe = c.sem("s3rpe")
        s_sm = c.sem("s3sm")
        s_cst = c.sem("s3cst")
        c.sm_cnt = 0

        with nc.Block() as blk:
            @blk.gpsimd
            def _(g):
                g.dma_start(gs_sb[:, :], c.t_gs.ap()).then_inc(s_gs.h, 16)
                g.wait_ge(s_gs.h, 16)

        for qt in range(4):
            with nc.Block() as blk:
                @blk.gpsimd
                def _(g, qt=qt):
                    for fc in range(NFI):
                        i = qt * NFI + fc
                        if i >= 4:
                            g.wait_ge(s_ope.h, i - 3)
                        src = bass.AP(c.t_wo, fc * 128,
                                      [[NFI * D, 128], [D, NFI], [1, 128]])
                        g.dma_start(wobuf[i % 4][:, :], src).then_inc(s_w.h, 16)

                @blk.sync
                def _(sy, qt=qt):
                    src = bass.AP(c.t_xown, qt * 512,
                                  [[TOK, 128], [128 * TOK, NFI], [1, 512]])
                    sy.dma_start(ypre[:, :], src).then_inc(s_x.h, 16)

                @blk.tensor
                def _(t, qt=qt):
                    t.wait_ge(s_x.h, 16 * (qt + 1))
                    for fc in range(NFI):
                        i = qt * NFI + fc
                        t.wait_ge(s_w.h, 16 * (i + 1))
                        t.wait_ge(s_yp.h, _cl(i - 1))
                        mm = None
                        for fi in range(NFI):
                            mm = t.matmul(
                                pb(i % 2),
                                wobuf[i % 4][:, fi * 128: fi * 128 + 128],
                                c.attn_fm[:, fi * TOK + qt * 512:
                                          fi * TOK + qt * 512 + 512],
                                start=(fi == 0), stop=(fi == NFI - 1))
                        mm.then_inc(s_ope.h, 1)

                @blk.vector
                def _(v, qt=qt):
                    for fc in range(NFI):
                        i = qt * NFI + fc
                        v.wait_ge(s_ope.h, i + 1)
                        v.tensor_add(ypre[:, fc * 512: fc * 512 + 512],
                                     pb(i % 2),
                                     ypre[:, fc * 512: fc * 512 + 512]
                                     ).then_inc(s_yp.h, 1)

            nrm0 = qt * 5
            with nc.Block() as blk:
                @blk.scalar
                def _(s, qt=qt):
                    mm = None
                    for fc in range(NFI):
                        mm = s.activation(y1f[:, fc * 512: fc * 512 + 512],
                                          ypre[:, fc * 512: fc * 512 + 512],
                                          AF.Square)
                    mm.then_inc(s_sq.h, 1)
                    s.wait_ge(s_nrm.h, nrm0 + 5)
                    s.activation(stat[0:1, 2560:3072], stat[0:1, 2048:2560],
                                 AF.Sqrt).then_inc(s_rstd.h, 1)
                    s.wait_ge(s_y1.h, qt + 1)
                    mm = None
                    for fc in range(NFI):
                        mm = s.copy(y1b[:, fc * 512: fc * 512 + 512],
                                    y1f[:, fc * 512: fc * 512 + 512])
                    mm.then_inc(s_y1b.h, 1)

                @blk.tensor
                def _(t, qt=qt):
                    for fc in range(NFI):
                        t.matmul(ps_sum[0:1, :], c.ones_col_f[:, :],
                                 ypre[:, fc * 512: fc * 512 + 512],
                                 start=(fc == 0), stop=(fc == NFI - 1))
                    t.wait_ge(s_sq.h, qt + 1)
                    mm = None
                    for fc in range(NFI):
                        mm = t.matmul(ps_sq[0:1, :], c.ones_col_f[:, :],
                                      y1f[:, fc * 512: fc * 512 + 512],
                                      start=(fc == 0), stop=(fc == NFI - 1))
                    mm.then_inc(s_stpe.h, 1)
                    t.wait_ge(s_nrm.h, nrm0 + 1)
                    t.matmul(pb(2), c.ones_row_f[:, :], stat[0:1, 0:512],
                             start=True, stop=True)
                    t.wait_ge(s_rstd.h, qt + 1)
                    t.matmul(pb(3), c.ones_row_f[:, :], stat[0:1, 2560:3072],
                             start=True, stop=True).then_inc(s_bc.h, 1)
                    t.wait_ge(s_y1.h, qt + 1)
                    mm = None
                    for fc in range(NFI):
                        for tc in range(4):
                            mm = t.matmul(ps_rt[:, tc * 17: tc * 17 + 17],
                                          y1f[:, fc * 512 + tc * 128:
                                              fc * 512 + tc * 128 + 128],
                                          gs_sb[:, fc * 17: fc * 17 + 17],
                                          start=(fc == 0 and tc == 0),
                                          stop=(fc == NFI - 1))
                    mm.then_inc(s_rpe.h, 1)

                @blk.vector
                def _(v, qt=qt):
                    v.wait_ge(s_stpe.h, qt + 1)
                    v.tensor_scalar_mul(stat[0:1, 0:512], ps_sum[0:1, :], INV_D)
                    v.tensor_scalar_mul(stat[0:1, 512:1024], ps_sq[0:1, :],
                                        INV_D).then_inc(s_nrm.h, 1)
                    v.wait_ge(s_nrm.h, nrm0 + 1)
                    v.tensor_mul(stat[0:1, 1024:1536], stat[0:1, 0:512],
                                 stat[0:1, 0:512]).then_inc(s_nrm.h, 1)
                    v.wait_ge(s_nrm.h, nrm0 + 2)
                    v.tensor_sub(stat[0:1, 1536:2048], stat[0:1, 512:1024],
                                 stat[0:1, 1024:1536]).then_inc(s_nrm.h, 1)
                    v.wait_ge(s_nrm.h, nrm0 + 3)
                    v.tensor_scalar_add(stat[0:1, 2048:2560],
                                        stat[0:1, 1536:2048],
                                        EPS).then_inc(s_nrm.h, 1)
                    v.wait_ge(s_nrm.h, nrm0 + 4)
                    v.reciprocal(stat[0:1, 2048:2560],
                                 stat[0:1, 2048:2560]).then_inc(s_nrm.h, 1)
                    v.wait_ge(s_bc.h, qt + 1)
                    mm = None
                    for fc in range(NFI):
                        mm = v.tensor_sub(y1f[:, fc * 512: fc * 512 + 512],
                                          ypre[:, fc * 512: fc * 512 + 512],
                                          pb(2))
                    mm.then_inc(s_sub.h, 1)
                    v.wait_ge(s_sub.h, qt + 1)
                    mm = None
                    for fc in range(NFI):
                        mm = v.tensor_mul(y1f[:, fc * 512: fc * 512 + 512],
                                          y1f[:, fc * 512: fc * 512 + 512],
                                          pb(3))
                    mm.then_inc(s_y1.h, 1)

                @blk.sync
                def _(sy, qt=qt):
                    sy.wait_ge(s_y1.h, qt + 1)
                    dst = bass.AP(c.t_y1, qt * 512,
                                  [[TOK, 128], [128 * TOK, NFI], [1, 512]])
                    sy.dma_start(dst, y1f[:, :]).then_inc(s_st.h, 16)
                    sy.wait_ge(s_y1b.h, qt + 1)
                    dstb = bass.AP(c.t_y1b, qt * 512,
                                   [[TOK, 128], [128 * TOK, NFI], [1, 512]])
                    sy.dma_start(dstb, y1b[:, :]).then_inc(s_st.h, 16)
                    sy.wait_ge(s_st.h, 32 * (qt + 1))

            steps = []
            tc_dma = []

            def V(fn):
                steps.append(("v", fn))

            def A(fn):
                steps.append(("s", fn))

            def T(fn):
                steps.append(("t", fn))

            for tc in range(4):
                r16 = ps_rt[:, tc * 17: tc * 17 + 16]
                rsg = ps_rt[:, tc * 17 + 16: tc * 17 + 17]
                V(lambda v, r16=r16: v.reduce_max(nmx[:, :], r16, axis=AX.X,
                                                  negate=True))
                A(lambda s, r16=r16: s.activation(probs[:, :], r16, AF.Exp,
                                                  bias=nmx[:, :],
                                                  accum_out=lsum[:, :]))
                A(lambda s, rsg=rsg: s.activation(sgc[:, :], rsg, AF.Sigmoid))
                V(lambda v: v.reciprocal(lin[:, :], lsum[:, :]))
                V(lambda v: v.tensor_scalar_mul(prn[:, :], probs[:, :],
                                                lin[:, :]))
                for it in range(4):
                    src = prn if it == 0 else wk4[it % 2]
                    V(lambda v, src=src: v.reduce_max(mx[:, :], src[:, :],
                                                      axis=AX.X))
                    if it < 3:
                        V(lambda v, src=src: v.tensor_scalar(
                            eq[:, :], src[:, :], mx[:, :], None, ALU.is_ge))
                        V(lambda v, src=src, it=it: v.scalar_tensor_tensor(
                            wk4[(it + 1) % 2][:, :], eq[:, :], -1e9, src[:, :],
                            ALU.mult, ALU.add))
                V(lambda v: v.tensor_scalar(eq[:, :], prn[:, :], mx[:, :],
                                            None, ALU.is_ge))
                V(lambda v: v.tensor_mul(coef_f[:, :], prn[:, :], eq[:, :]))
                V(lambda v: v.tensor_copy(coef_bf[:, 0:16], coef_f[:, :]))
                V(lambda v: v.tensor_copy(coef_bf[:, 16:17], sgc[:, :]))
                tcg = qt * 4 + tc
                T(lambda t: t.transpose(tr_ps[0:17, 0:128], coef_bf[:, 0:17],
                                        c.id_bf[:, :]))

                def cp(s, tcg=tcg):
                    s.wait_ge(s_cst.h, 16 * _cl(tcg - 1))
                    return s.copy(coefT[tcg % 2][0:17, :],
                                  tr_ps[0:17, 0:128])
                A(cp)
                copy_end = len(steps)
                tc_dma.append((tcg, qt, tc, copy_end))

            base = c.sm_cnt
            with nc.Block() as blk:
                @blk.vector
                def _(v, steps=steps, base=base):
                    for idx, (eng, fn) in enumerate(steps):
                        if eng == "v":
                            v.wait_ge(s_sm.h, base + idx)
                            fn(v).then_inc(s_sm.h, 1)

                @blk.scalar
                def _(s, steps=steps, base=base):
                    for idx, (eng, fn) in enumerate(steps):
                        if eng == "s":
                            s.wait_ge(s_sm.h, base + idx)
                            fn(s).then_inc(s_sm.h, 1)

                @blk.tensor
                def _(t, steps=steps, base=base):
                    for idx, (eng, fn) in enumerate(steps):
                        if eng == "t":
                            t.wait_ge(s_sm.h, base + idx)
                            fn(t).then_inc(s_sm.h, 1)

                @blk.sync
                def _(sy, tc_dma=tc_dma, base=base):
                    for tcg, qt2, tc2, copy_end in tc_dma:
                        sy.wait_ge(s_sm.h, base + copy_end)
                        dst = bass.AP(c.t_coef, tcg * 128,
                                      [[TOK, 17], [1, 128]])
                        sy.dma_start(dst, coefT[tcg % 2][0:17, :]
                                     ).then_inc(s_cst.h, 16)
                    sy.wait_ge(s_cst.h, 16 * (tc_dma[-1][0] + 1))
            c.sm_cnt += len(steps)


def _p5(c):
    nc = c.nc
    s_y1bh = c.sem("s5y")
    s_w13 = c.sem("s5w13")
    s_gupe = c.sem("s5gupe")
    s_silu = c.sem("s5silu")
    s_uc = c.sem("s5uc")
    s_h = c.sem("s5h")
    s_crepd = c.sem("s5crepd")
    s_w2 = c.sem("s5w2")
    s_w2pe = c.sem("s5w2pe")
    s_acc = c.sem("s5acc")
    s_res = c.sem("s5res")
    s_add = c.sem("s5add")
    s_sq = c.sem("s5sq")
    s_stpe = c.sem("s5stpe")
    s_nrm = c.sem("s5nrm")
    s_rstd = c.sem("s5rstd")
    s_bc = c.sem("s5bc")
    s_sub = c.sem("s5sub")
    s_y2 = c.sem("s5y2")
    s_out = c.sem("s5out")

    for th in range(2):
        with ExitStack() as thc:
            sbl = lambda n, sh, dt: thc.enter_context(nc.sbuf_tensor(n, sh, dt))
            y1bh = sbl(f"y1bh{th}", [128, NFI * 1024], BF16)
            acc = sbl(f"acc5{th}", [128, NFI * 1024], F32)
            with ExitStack() as ectx:
                sbe = lambda n, sh, dt: ectx.enter_context(
                    nc.sbuf_tensor(n, sh, dt))
                w13buf = [sbe(f"w13b{th}_{i}", [128, 2 * NFI * 128], BF16)
                          for i in range(2)]
                w2buf = [sbe(f"w2b{th}_{i}", [128, NIC * 128], BF16)
                         for i in range(4)]
                h_sb = sbe(f"h5_{th}", [128, NIC * 1024], BF16)
                sgs = [sbe(f"sg5{th}_{i}", [128, 512], BF16) for i in range(4)]
                ucs = [sbe(f"uc5{th}_{i}", [128, 512], BF16) for i in range(4)]
                crep_b = [sbe(f"crep5_{th}_{i}", [128, 1024], BF16)
                          for i in range(2)]
                ps = ectx.enter_context(nc.psum_tensor(f"ps5_{th}",
                                                       [128, 4096], F32))
                pb = lambda b: ps[:, b * 512:(b + 1) * 512]

                with nc.Block() as blk:
                    @blk.sync
                    def _(sy, th=th):
                        src = bass.AP(c.t_y1b, th * 1024,
                                      [[TOK, 128], [128 * TOK, NFI], [1, 1024]])
                        sy.dma_start(y1bh[:, :], src).then_inc(s_y1bh.h, 16)
                        for e in range(NEXP):
                            eg = th * NEXP + e
                            if eg >= 2:
                                sy.wait_ge(s_uc.h, 22 * (eg - 1))
                            src = bass.AP(c.t_coef,
                                          min(e, 16) * TOK + th * 1024,
                                          [[0, 128], [1, 1024]])
                            sy.dma_start(crep_b[e % 2][:, :],
                                         src).then_inc(s_crepd.h, 16)

                    @blk.gpsimd
                    def _(g, th=th):
                        for e in range(NEXP):
                            eg = th * NEXP + e
                            for ic in range(NIC):
                                gic = eg * NIC + ic
                                if gic >= 2:
                                    g.wait_ge(s_gupe.h, gic - 1)
                                src = bass.AP(
                                    c.t_w13,
                                    (e * NIC + ic) * 128 * 2 * NFI * 128,
                                    [[2 * NFI * 128, 128], [1, 2 * NFI * 128]])
                                g.dma_start(w13buf[ic % 2][:, :],
                                            src).then_inc(s_w13.h, 16)
                            for fc in range(NFI):
                                gfc = eg * NFI + fc
                                if gfc >= 4:
                                    g.wait_ge(s_w2pe.h, 2 * (gfc - 3))
                                src = bass.AP(
                                    c.t_w2,
                                    (e * NFI + fc) * 128 * NIC * 128,
                                    [[NIC * 128, 128], [1, NIC * 128]])
                                g.dma_start(w2buf[fc % 4][:, :],
                                            src).then_inc(s_w2.h, 16)

                    @blk.tensor
                    def _(t, th=th):
                        t.wait_ge(s_y1bh.h, 16 * (th + 1))
                        for e in range(NEXP):
                            eg = th * NEXP + e
                            t.wait_ge(s_silu.h, 22 * eg)
                            t.wait_ge(s_uc.h, 22 * eg)
                            for ic in range(NIC):
                                gic = eg * NIC + ic
                                p4 = 4 * (ic % 2)
                                t.wait_ge(s_w13.h, 16 * (gic + 1))
                                if ic == 0:
                                    t.wait_ge(s_acc.h, _cl(32 * eg - 4))
                                elif ic == 1:
                                    t.wait_ge(s_acc.h, _cl(32 * eg))
                                else:
                                    t.wait_ge(s_silu.h, 2 * (gic - 1))
                                    t.wait_ge(s_uc.h, 2 * (gic - 1))
                                mm = None
                                for gu in range(2):
                                    for fi in range(NFI):
                                        for tt2 in range(2):
                                            mm = t.matmul(
                                                pb(p4 + 2 * gu + tt2),
                                                w13buf[ic % 2][
                                                    :, (gu * NFI + fi) * 128:
                                                    (gu * NFI + fi) * 128 + 128],
                                                y1bh[:, fi * 1024 + tt2 * 512:
                                                     fi * 1024 + tt2 * 512 + 512],
                                                start=(fi == 0),
                                                stop=(fi == NFI - 1))
                                mm.then_inc(s_gupe.h, 1)
                            t.wait_ge(s_h.h, 22 * (eg + 1))
                            for fc in range(NFI):
                                gfc = eg * NFI + fc
                                t.wait_ge(s_w2.h, 16 * (gfc + 1))
                                for tt2 in range(2):
                                    gw = 2 * gfc + tt2
                                    bk = gw % 8
                                    t.wait_ge(s_acc.h, _cl(gw - 7))
                                    mm = None
                                    for ic in range(NIC):
                                        mm = t.matmul(
                                            pb(bk),
                                            w2buf[fc % 4][:, ic * 128:
                                                          ic * 128 + 128],
                                            h_sb[:, ic * 1024 + tt2 * 512:
                                                 ic * 1024 + tt2 * 512 + 512],
                                            start=(ic == 0), stop=(ic == NIC - 1))
                                    mm.then_inc(s_w2pe.h, 1)

                    @blk.scalar
                    def _(s, th=th):
                        for e in range(NEXP):
                            eg = th * NEXP + e
                            for ic in range(NIC):
                                gic = eg * NIC + ic
                                p4 = 4 * (ic % 2)
                                s.wait_ge(s_gupe.h, gic + 1)
                                s.wait_ge(s_h.h, _cl(2 * (gic - 1)))
                                for tt2 in range(2):
                                    s.activation(sgs[(2 * gic + tt2) % 4][:, :],
                                                 pb(p4 + tt2), AF.Silu
                                                 ).then_inc(s_silu.h, 1)
                            if e == 0:
                                for fc in range(NFI):
                                    gfc = eg * NFI + fc
                                    for tt2 in range(2):
                                        gw = 2 * gfc + tt2
                                        s.wait_ge(s_w2pe.h, gw + 1)
                                        s.copy(acc[:, fc * 1024 + tt2 * 512:
                                                   fc * 1024 + tt2 * 512 + 512],
                                               pb(gw % 8)).then_inc(s_acc.h, 1)

                    @blk.vector
                    def _(v, th=th):
                        for e in range(NEXP):
                            eg = th * NEXP + e
                            for ic in range(NIC):
                                gic = eg * NIC + ic
                                p4 = 4 * (ic % 2)
                                v.wait_ge(s_gupe.h, gic + 1)
                                if ic == 0:
                                    v.wait_ge(s_crepd.h, 16 * (eg + 1))
                                if ic < 2:
                                    v.wait_ge(s_w2pe.h, 32 * eg)
                                v.wait_ge(s_h.h, _cl(2 * (gic - 1)))
                                for tt2 in range(2):
                                    v.tensor_mul(ucs[(2 * gic + tt2) % 4][:, :],
                                                 crep_b[e % 2][:, tt2 * 512:
                                                               tt2 * 512 + 512],
                                                 pb(p4 + 2 + tt2)
                                                 ).then_inc(s_uc.h, 1)
                                v.wait_ge(s_uc.h, 2 * (gic + 1))
                                v.wait_ge(s_silu.h, 2 * (gic + 1))
                                for tt2 in range(2):
                                    v.tensor_mul(h_sb[:, ic * 1024 + tt2 * 512:
                                                      ic * 1024 + tt2 * 512 + 512],
                                                 sgs[(2 * gic + tt2) % 4][:, :],
                                                 ucs[(2 * gic + tt2) % 4][:, :]
                                                 ).then_inc(s_h.h, 1)
                            if e > 0:
                                for fc in range(NFI):
                                    gfc = eg * NFI + fc
                                    for tt2 in range(2):
                                        gw = 2 * gfc + tt2
                                        v.wait_ge(s_w2pe.h, gw + 1)
                                        if fc == 0 and tt2 == 0:
                                            v.wait_ge(s_acc.h, 32 * eg)
                                        sl = acc[:, fc * 1024 + tt2 * 512:
                                                 fc * 1024 + tt2 * 512 + 512]
                                        v.tensor_add(sl, pb(gw % 8),
                                                     sl).then_inc(s_acc.h, 1)

            with ExitStack() as lctx:
                sbn = lambda n, sh, dt: lctx.enter_context(
                    nc.sbuf_tensor(n, sh, dt))
                ypre2 = sbn(f"ypre2_{th}", [128, NFI * 512], F32)
                y1res = sbn(f"y1res_{th}", [128, NFI * 512], F32)
                stat2 = sbn(f"stat2_{th}", [1, 512 * 6], F32)
                ps2 = lctx.enter_context(nc.psum_tensor(f"ps5l_{th}",
                                                        [128, 2048], F32))
                ps_sum2 = lctx.enter_context(nc.psum_tensor(f"ps_sum5_{th}",
                                                            [1, 512], F32))
                ps_sq2 = lctx.enter_context(nc.psum_tensor(f"ps_sq5_{th}",
                                                           [1, 512], F32))
                pb2 = lambda b: ps2[:, b * 512:(b + 1) * 512]
                for qt2 in range(2):
                    m = th * 2 + qt2
                    nrm0 = m * 5
                    with nc.Block() as blk:
                        @blk.sync
                        def _(sy, th=th, qt2=qt2, m=m):
                            for ci in range(4):
                                src = bass.AP(
                                    c.t_y1,
                                    th * 1024 + qt2 * 512
                                    + 4 * ci * 128 * TOK,
                                    [[TOK, 128], [128 * TOK, 4], [1, 512]])
                                sy.dma_start(
                                    y1res[:, 4 * ci * 512:
                                          4 * ci * 512 + 4 * 512],
                                    src).then_inc(s_res.h, 16)

                        @blk.vector
                        def _(v, th=th, qt2=qt2, m=m):
                            mm = None
                            for fc in range(NFI):
                                if fc % 4 == 0:
                                    v.wait_ge(s_res.h,
                                              16 * (4 * m + fc // 4 + 1))
                                mm = v.tensor_add(
                                    ypre2[:, fc * 512: fc * 512 + 512],
                                    acc[:, fc * 1024 + qt2 * 512:
                                        fc * 1024 + qt2 * 512 + 512],
                                    y1res[:, fc * 512: fc * 512 + 512])
                            mm.then_inc(s_add.h, 1)

                    with nc.Block() as blk:
                        @blk.scalar
                        def _(s, th=th, qt2=qt2, m=m):
                            mm = None
                            for fc in range(NFI):
                                mm = s.activation(
                                    y1res[:, fc * 512: fc * 512 + 512],
                                    ypre2[:, fc * 512: fc * 512 + 512],
                                    AF.Square)
                            mm.then_inc(s_sq.h, 1)
                            s.wait_ge(s_nrm.h, nrm0 + 5)
                            s.activation(stat2[0:1, 2560:3072],
                                         stat2[0:1, 2048:2560],
                                         AF.Sqrt).then_inc(s_rstd.h, 1)

                        @blk.tensor
                        def _(t, th=th, qt2=qt2, m=m):
                            for fc in range(NFI):
                                t.matmul(ps_sum2[0:1, :], c.ones_col_f[:, :],
                                         ypre2[:, fc * 512: fc * 512 + 512],
                                         start=(fc == 0), stop=(fc == NFI - 1))
                            t.wait_ge(s_sq.h, m + 1)
                            mm = None
                            for fc in range(NFI):
                                mm = t.matmul(ps_sq2[0:1, :],
                                              c.ones_col_f[:, :],
                                              y1res[:, fc * 512: fc * 512 + 512],
                                              start=(fc == 0),
                                              stop=(fc == NFI - 1))
                            mm.then_inc(s_stpe.h, 1)
                            t.wait_ge(s_nrm.h, nrm0 + 1)
                            t.matmul(pb2(0), c.ones_row_f[:, :],
                                     stat2[0:1, 0:512], start=True, stop=True)
                            t.wait_ge(s_rstd.h, m + 1)
                            t.matmul(pb2(1), c.ones_row_f[:, :],
                                     stat2[0:1, 2560:3072], start=True,
                                     stop=True).then_inc(s_bc.h, 1)

                        @blk.vector
                        def _(v, th=th, qt2=qt2, m=m):
                            v.wait_ge(s_stpe.h, m + 1)
                            v.tensor_scalar_mul(stat2[0:1, 0:512],
                                                ps_sum2[0:1, :], INV_D)
                            v.tensor_scalar_mul(stat2[0:1, 512:1024],
                                                ps_sq2[0:1, :],
                                                INV_D).then_inc(s_nrm.h, 1)
                            v.wait_ge(s_nrm.h, nrm0 + 1)
                            v.tensor_mul(stat2[0:1, 1024:1536],
                                         stat2[0:1, 0:512],
                                         stat2[0:1, 0:512]).then_inc(s_nrm.h, 1)
                            v.wait_ge(s_nrm.h, nrm0 + 2)
                            v.tensor_sub(stat2[0:1, 1536:2048],
                                         stat2[0:1, 512:1024],
                                         stat2[0:1, 1024:1536]
                                         ).then_inc(s_nrm.h, 1)
                            v.wait_ge(s_nrm.h, nrm0 + 3)
                            v.tensor_scalar_add(stat2[0:1, 2048:2560],
                                                stat2[0:1, 1536:2048],
                                                EPS).then_inc(s_nrm.h, 1)
                            v.wait_ge(s_nrm.h, nrm0 + 4)
                            v.reciprocal(stat2[0:1, 2048:2560],
                                         stat2[0:1, 2048:2560]
                                         ).then_inc(s_nrm.h, 1)
                            v.wait_ge(s_bc.h, m + 1)
                            mm = None
                            for fc in range(NFI):
                                mm = v.tensor_sub(
                                    ypre2[:, fc * 512: fc * 512 + 512],
                                    ypre2[:, fc * 512: fc * 512 + 512], pb2(0))
                            mm.then_inc(s_sub.h, 1)
                            v.wait_ge(s_sub.h, m + 1)
                            mm = None
                            for fc in range(NFI):
                                mm = v.tensor_mul(
                                    ypre2[:, fc * 512: fc * 512 + 512],
                                    ypre2[:, fc * 512: fc * 512 + 512], pb2(1))
                            mm.then_inc(s_y2.h, 1)

                        @blk.sync
                        def _(sy, th=th, qt2=qt2, m=m):
                            sy.wait_ge(s_y2.h, m + 1)
                            dst = bass.AP(c.t_out, th * 1024 + qt2 * 512,
                                          [[TOK, 128], [128 * TOK, NFI],
                                           [1, 512]])
                            sy.dma_start(dst, ypre2[:, :]).then_inc(s_out.h, 16)
                            sy.wait_ge(s_out.h, 16 * (m + 1))


_NC_CACHE = {}


def kernel(**inputs):
    per_core = _host_prep(inputs)
    key = (LAST_PHASE, tuple(sorted(DBG)))
    if key not in _NC_CACHE:
        _NC_CACHE[key] = build_nc()
    nc = _NC_CACHE[key]
    res = run_bass_kernel_spmd(nc, per_core, core_ids=list(range(8)))
    return _assemble([r["out"] for r in res.results])


if __name__ == "__main__":
    import reference as _ref
    inp = _ref.setup_inputs()
    y = kernel(**{k: np.asarray(v) for k, v in inp.items()})
    ref = np.asarray(_ref.reference(**inp))
    num = float(((y - ref) ** 2).mean())
    den = float((ref ** 2).mean())
    print("resvar", num / den)
